# version 03
# baseline (speedup 1.0000x reference)
import numpy as np
import ml_dtypes
import concourse.bass as bass
import concourse.mybir as mybir
from concourse.ap import AP
from concourse.bass_utils import run_bass_kernel_spmd
from contextlib import ExitStack



"""Bass kernel builders for the mk_prb_d2 megakernel.

Layout conventions:
 - activations d-major in SBUF: [d-chunk(128 partitions), tokens on free]
 - weights host-pretransposed to [in, out] so they serve as lhsT directly
 - per-core rotation by start = 2048*(core%2); batch b = core//2
Sync rules (hardware-verified):
 - cross-engine RAW: producer.drain().then_inc(sem); consumer wait_ge
   (PE: then_inc directly on last matmul; DMA: then_inc 16)
 - any AP-scalar/bias ("ptr") operand must be guarded by drain+sem from its
   producer even on the same engine
"""
import numpy as np
import ml_dtypes
import concourse.bass as bass
import concourse.mybir as mybir
from concourse.ap import AP

F32 = mybir.dt.float32
BF16 = mybir.dt.bfloat16
AF = mybir.ActivationFunctionType
ALU = mybir.AluOpType
AX = mybir.AxisListType

T = 4096
E = 1024
HALF = 2048


class Sem:
    def __init__(self, nc, name):
        self.h = nc.alloc_semaphore(name)
        self.n = 0

    def inc(self, k=1):
        self.n += k
        return self.n


class Ctx:
    def __init__(self, nc):
        self.nc = nc
        self.sems = {}
        self.dram = {}

    def sem(self, name):
        if name not in self.sems:
            self.sems[name] = Sem(self.nc, name)
        return self.sems[name]

    def dt(self, name, shape, dtype, kind="Internal"):
        if name not in self.dram:
            self.dram[name] = self.nc.dram_tensor(name, list(shape), dtype, kind=kind)
        return self.dram[name]


def bfarr(x):
    return np.ascontiguousarray(np.asarray(x, dtype=ml_dtypes.bfloat16))


def f32arr(x):
    return np.ascontiguousarray(np.asarray(x), dtype=np.float32)



def build_l1ffn(cx, io_kind="ExternalInput", out_kind="ExternalOutput"):
    nc = cx.nc
    xn1b = cx.dt("xn1b", [8, 128, HALF], BF16, kind=io_kind)
    x1d = cx.dt("x1d", [8, 128, HALF], F32, kind=io_kind)
    ys4d = cx.dt("ys4d", [8, 128, HALF], F32, kind=io_kind)
    w1T = cx.dt("l1_w1T", [E, 4096], F32, kind=io_kind)
    w2T = cx.dt("l1_w2T", [E, 4096], F32, kind=io_kind)
    h1T = cx.dt("l1_h1T", [4096, 4096], F32, kind=io_kind)
    h2T = cx.dt("l1_h2T", [4096, 4096], F32, kind=io_kind)
    w3T = cx.dt("l1_w3T", [4096, E], F32, kind=io_kind)
    scal = cx.dt("l1_scales", [128, 5 * 32], F32, kind=io_kind)
    outd = cx.dt("outd", [8, 128, HALF], F32, kind=out_kind)

    TT = 1024
    sl = cx.sem("l1_load")
    sw = cx.sem("l1_wload")
    swc = cx.sem("l1_wcast")
    sm = cx.sem("l1_mm")
    ss = cx.sem("l1_sc")
    sv = cx.sem("l1_vec")
    sst = cx.sem("l1_store")

    with (
        nc.sbuf_tensor("l1_xnb", [128, 8 * TT], BF16) as xnb,
        nc.sbuf_tensor("l1_scol", [128, 5 * 32], F32) as scol,
        nc.sbuf_tensor("l1_wstage", [128, 4 * 512], F32) as wstage,
        nc.sbuf_tensor("l1_wcb", [128, 16 * 512], BF16) as wcb,
        nc.sbuf_tensor("l1_wcb2", [128, 16 * 512], BF16) as wcb2,
        nc.sbuf_tensor("l1_hbuf", [128, 32 * TT], BF16) as hbuf,
        nc.sbuf_tensor("l1_gbuf", [128, 32 * TT], BF16) as gbuf,
        nc.sbuf_tensor("l1_u1", [128, TT], BF16) as u1sb,
        nc.sbuf_tensor("l1_u2", [128, TT], BF16) as u2sb,
        nc.sbuf_tensor("l1_fin", [128, 3 * TT], F32) as fin,
        nc.sbuf_tensor("l1_finb", [128, TT], BF16) as finb,
        nc.psum_tensor("l1_ps0", [128, TT], F32) as ps0,
        nc.psum_tensor("l1_ps1", [128, TT], F32) as ps1,
        nc.psum_tensor("l1_ps2", [128, TT], F32) as ps2,
        nc.psum_tensor("l1_ps3", [128, TT], F32) as ps3,
    ):
        nc.sync.dma_start(out=scol[:], in_=scal.ap()).then_inc(sl.h, 16)
        sl.inc(16)
        nc.scalar.wait_ge(sl.h, sl.n)

        def load_chunks(matT, ob, kcs, dst, slot0):
            for i in range(0, len(kcs), 4):
                grp = kcs[i:i + 4]
                for g, kc in enumerate(grp):
                    nc.sync.dma_start(
                        out=wstage[:, g * 512:(g + 1) * 512],
                        in_=matT.ap()[kc * 128:(kc + 1) * 128, ob * 512:(ob + 1) * 512],
                    ).then_inc(sw.h, 16)
                    sw.inc(16)
                nc.vector.wait_ge(sw.h, sw.n)
                for g, kc in enumerate(grp):
                    nc.vector.tensor_copy(
                        dst[:, (slot0 + i + g) * 512:(slot0 + i + g + 1) * 512],
                        wstage[:, g * 512:(g + 1) * 512])
                nc.vector.drain().then_inc(swc.h, 1)
                swc.inc(1)
                nc.sync.wait_ge(swc.h, swc.n)

        def mm_block(src, wbuf_slots, psum, kcs, osub, nfree, s_then=None):
            last = None
            n_k = len(kcs)
            for i, (wbuf, slot, kc) in enumerate(wbuf_slots):
                st = i == 0
                sp = i == n_k - 1
                lhs = wbuf[:, slot * 512 + osub * 128:slot * 512 + (osub + 1) * 128]
                for th in range(nfree // 512):
                    last = nc.tensor.matmul(
                        psum[:, th * 512:(th + 1) * 512], lhs,
                        src[:, kc * TT + th * 512:kc * TT + (th + 1) * 512],
                        start=st and th is not None and st, stop=sp)
            return last

        for tt in range(2):
            t0 = tt * TT
            for ch in range(8):
                nc.sync.dma_start(
                    out=xnb[:, ch * TT:(ch + 1) * TT],
                    in_=xn1b.ap()[ch, :, t0:t0 + TT],
                ).then_inc(sl.h, 16)
                sl.inc(16)

            for ob in range(8):
                load_chunks(w1T, ob, list(range(8)), wcb, 0)
                load_chunks(w2T, ob, list(range(8)), wcb2, 0)
                nc.tensor.wait_ge(swc.h, swc.n)
                nc.tensor.wait_ge(sl.h, sl.n)
                for osub in range(4):
                    och = ob * 4 + osub
                    pu1 = [ps0, ps1][osub % 2]
                    pu2 = [ps2, ps3][osub % 2]
                    if osub >= 2:
                        nc.tensor.wait_ge(sv.h, sv.n)
                    for i, kc in enumerate(range(8)):
                        st, sp = kc == 0, kc == 7
                        l1_ = wcb[:, kc * 512 + osub * 128:kc * 512 + (osub + 1) * 128]
                        l2_ = wcb2[:, kc * 512 + osub * 128:kc * 512 + (osub + 1) * 128]
                        for th in range(2):
                            nc.tensor.matmul(pu1[:, th * 512:(th + 1) * 512], l1_,
                                             xnb[:, kc * TT + th * 512:kc * TT + (th + 1) * 512],
                                             start=st, stop=sp)
                            mm = nc.tensor.matmul(pu2[:, th * 512:(th + 1) * 512], l2_,
                                                  xnb[:, kc * TT + th * 512:kc * TT + (th + 1) * 512],
                                                  start=st, stop=sp)
                    mm.then_inc(sm.h, 1)
                    sm.inc(1)
                    nc.scalar.wait_ge(sm.h, sm.n)
                    nc.scalar.wait_ge(sv.h, sv.n)
                    nc.scalar.activation(u1sb[:], pu1[:], AF.Silu,
                                         scale=scol[:, och:och + 1])
                    nc.scalar.activation(u2sb[:], pu2[:], AF.Identity,
                                         scale=scol[:, 32 + och:32 + och + 1])
                    nc.scalar.drain().then_inc(ss.h, 1)
                    ss.inc(1)
                    nc.vector.wait_ge(ss.h, ss.n)
                    nc.vector.tensor_mul(hbuf[:, och * TT:(och + 1) * TT], u1sb[:], u2sb[:])
                    nc.vector.drain().then_inc(sv.h, 1)
                    sv.inc(1)
                nc.tensor.drain().then_inc(sm.h, 1)
                sm.inc(1)
                nc.sync.wait_ge(sm.h, sm.n)

            for stage, (matT, sbase, src, dst) in enumerate(
                    [(h1T, 2, hbuf, gbuf), (h2T, 3, gbuf, hbuf)]):
                nc.tensor.wait_ge(sv.h, sv.n)
                for ob in range(8):
                    load_chunks(matT, ob, list(range(16)), wcb, 0)
                    load_chunks(matT, ob, list(range(16, 32)), wcb2, 0)
                    nc.tensor.wait_ge(swc.h, swc.n)
                    for osub in range(4):
                        och = ob * 4 + osub
                        pg = [ps0, ps1][osub % 2]
                        if osub >= 2:
                            nc.tensor.wait_ge(ss.h, ss.n)
                        for kc in range(32):
                            st, sp = kc == 0, kc == 31
                            wb_ = wcb if kc < 16 else wcb2
                            slot = kc % 16
                            lhs = wb_[:, slot * 512 + osub * 128:slot * 512 + (osub + 1) * 128]
                            for th in range(2):
                                mm = nc.tensor.matmul(
                                    pg[:, th * 512:(th + 1) * 512], lhs,
                                    src[:, kc * TT + th * 512:kc * TT + (th + 1) * 512],
                                    start=st, stop=sp)
                        mm.then_inc(sm.h, 1)
                        sm.inc(1)
                        nc.scalar.wait_ge(sm.h, sm.n)
                        nc.scalar.activation(dst[:, och * TT:(och + 1) * TT], pg[:],
                                             AF.Gelu,
                                             scale=scol[:, sbase * 32 + och:sbase * 32 + och + 1])
                        nc.scalar.drain().then_inc(ss.h, 1)
                        ss.inc(1)
                    nc.tensor.drain().then_inc(sm.h, 1)
                    sm.inc(1)
                    nc.sync.wait_ge(sm.h, sm.n)
                nc.scalar.drain().then_inc(sv.h, 1)
                sv.inc(1)
                nc.tensor.wait_ge(sv.h, sv.n)

            for ob in range(2):
                load_chunks(w3T, ob, list(range(16)), wcb, 0)
                load_chunks(w3T, ob, list(range(16, 32)), wcb2, 0)
                nc.tensor.wait_ge(swc.h, swc.n)
                for osub in range(4):
                    och = ob * 4 + osub
                    pg = [ps0, ps1][osub % 2]
                    if osub >= 2:
                        nc.tensor.wait_ge(sv.h, sv.n)
                    for kc in range(32):
                        st, sp = kc == 0, kc == 31
                        wb_ = wcb if kc < 16 else wcb2
                        slot = kc % 16
                        lhs = wb_[:, slot * 512 + osub * 128:slot * 512 + (osub + 1) * 128]
                        for th in range(2):
                            mm = nc.tensor.matmul(
                                pg[:, th * 512:(th + 1) * 512], lhs,
                                hbuf[:, kc * TT + th * 512:kc * TT + (th + 1) * 512],
                                start=st, stop=sp)
                    mm.then_inc(sm.h, 1)
                    sm.inc(1)
                    nc.sync.dma_start(out=fin[:, TT:2 * TT],
                                      in_=x1d.ap()[och, :, t0:t0 + TT]).then_inc(sl.h, 16)
                    sl.inc(16)
                    nc.sync.dma_start(out=fin[:, 2 * TT:3 * TT],
                                      in_=ys4d.ap()[och, :, t0:t0 + TT]).then_inc(sl.h, 16)
                    sl.inc(16)
                    nc.sync.dma_start(out=finb[:, 0:TT],
                                      in_=xn1b.ap()[och, :, t0:t0 + TT]).then_inc(sl.h, 16)
                    sl.inc(16)
                    nc.scalar.wait_ge(sm.h, sm.n)
                    nc.scalar.activation(fin[:, 0:TT], pg[:], AF.Identity,
                                         scale=scol[:, 4 * 32 + och:4 * 32 + och + 1])
                    nc.scalar.drain().then_inc(ss.h, 1)
                    ss.inc(1)
                    nc.vector.wait_ge(ss.h, ss.n)
                    nc.vector.wait_ge(sl.h, sl.n)
                    nc.vector.tensor_add(fin[:, 0:TT], fin[:, 0:TT], fin[:, TT:2 * TT])
                    nc.vector.tensor_add(fin[:, 0:TT], fin[:, 0:TT], fin[:, 2 * TT:3 * TT])
                    nc.vector.scalar_tensor_tensor(fin[:, 0:TT], finb[:, 0:TT],
                                                   scol[:, 144 + och:145 + och],
                                                   fin[:, 0:TT], ALU.mult, ALU.add)
                    nc.vector.drain().then_inc(sv.h, 1)
                    sv.inc(1)
                    nc.gpsimd.wait_ge(sv.h, sv.n)
                    nc.gpsimd.dma_start(out=outd.ap()[och, :, t0:t0 + TT],
                                        in_=fin[:, 0:TT]).then_inc(sst.h, 16)
                    sst.inc(16)
                    nc.scalar.wait_ge(sst.h, sst.n)
                    nc.sync.wait_ge(sst.h, sst.n)
                nc.tensor.drain().then_inc(sm.h, 1)
                sm.inc(1)
                nc.sync.wait_ge(sm.h, sm.n)
        nc.gpsimd.wait_ge(sst.h, sst.n)
    return outd



def build_s4d(cx, io_kind="ExternalInput", out_kind="ExternalOutput", dbg=False, serial=False):
    nc = cx.nc
    xg = cx.dt("xg", [T, E], BF16, kind=io_kind)
    logdt = cx.dt("s4_logdt", [8, 128], F32, kind=io_kind)
    Bm = cx.dt("s4_B", [8, 128, 64], F32, kind=io_kind)
    Cm = cx.dt("s4_C", [8, 128, 64], F32, kind=io_kind)
    ys4f = cx.dt("ys4f", [8, 128, T], F32, kind=out_kind)
    dtA_d = cx.dt("s4_dtad", [128, 64], F32)
    w_d = cx.dt("s4_wd", [128, 64], F32)
    dtAT_d = cx.dt("s4_dtat", [64, 128], F32)
    wT_d = cx.dt("s4_wt", [64, 128], F32)
    lamLT_d = cx.dt("s4_lamlt", [64, 128], F32)
    kdram = cx.dt("s4_kdram", [64, 128, 192], F32)
    Arow = nc.inline_tensor(f32arr(np.tile(-0.5 * np.arange(1, 65, dtype=np.float32), (128, 1))), "c_arow").ap()
    invA = nc.inline_tensor(f32arr(np.tile(1.0 / (-0.5 * np.arange(1, 65, dtype=np.float32)), (128, 1))), "c_inva").ap()
    kv = np.zeros((128, 64 * 64), np.float32)
    for k in range(64):
        kv[:, k * 64:(k + 1) * 64] = k
    kvals = nc.inline_tensor(f32arr(kv), "c_kvals").ap()
    m63 = np.zeros((128, 1), np.float32)
    m63[:64, 0] = 63 - np.arange(64)
    m63[64:, 0] = 63 - np.arange(64)
    m63col = nc.inline_tensor(f32arr(m63), "c_m63").ap()
    lp1 = np.zeros((128, 64 * 64), np.float32)
    for l in range(64):
        lp1[:, l * 64:(l + 1) * 64] = l + 1
    lp1c = nc.inline_tensor(f32arr(lp1), "c_lp1").ap()
    ident = nc.inline_tensor(f32arr(np.eye(128, dtype=np.float32)), "c_ident_f32").ap()

    sl = cx.sem("s4_load")
    sb = cx.sem("s4_bounce")
    sbs = cx.sem("s4_bounce_s")
    sc = cx.sem("s4_scal")
    svv = cx.sem("s4_vec")
    smm = cx.sem("s4_mm")
    sst = cx.sem("s4_store")

    from contextlib import ExitStack
    es4 = ExitStack()
    idt = es4.enter_context(nc.sbuf_tensor("s4_ident", [128, 128], F32))
    cArow = es4.enter_context(nc.sbuf_tensor("s4_carow", [128, 64], F32))
    cInvA = es4.enter_context(nc.sbuf_tensor("s4_cinva", [128, 64], F32))
    cKvals = es4.enter_context(nc.sbuf_tensor("s4_ckv", [128, 4096], F32))
    cM63 = es4.enter_context(nc.sbuf_tensor("s4_cm63", [128, 1], F32))
    cLp1 = es4.enter_context(nc.sbuf_tensor("s4_clp1", [128, 4096], F32))
    atf = es4.enter_context(nc.sbuf_tensor("s4_atf", [128, 64], F32))
    ldt = es4.enter_context(nc.sbuf_tensor("s4_ldt", [128, 1], F32))
    dtc = es4.enter_context(nc.sbuf_tensor("s4_dt", [128, 1], F32))
    bc2 = es4.enter_context(nc.sbuf_tensor("s4_BC", [128, 128], F32))
    dta = es4.enter_context(nc.sbuf_tensor("s4_dta", [128, 64], F32))
    wv = es4.enter_context(nc.sbuf_tensor("s4_w", [128, 64], F32))
    laml = es4.enter_context(nc.sbuf_tensor("s4_laml", [128, 64], F32))
    big = es4.enter_context(nc.sbuf_tensor("s4_big", [128, 4096], F32))
    big2 = es4.enter_context(nc.sbuf_tensor("s4_big2", [128, 4096], F32))
    tpad = es4.enter_context(nc.sbuf_tensor("s4_tpad", [128, 128], F32))
    ktb = es4.enter_context(nc.sbuf_tensor("s4_kt", [128, 4096], BF16))
    p1b = es4.enter_context(nc.sbuf_tensor("s4_p1", [128, 4096], BF16))
    p2b = es4.enter_context(nc.sbuf_tensor("s4_p2", [128, 4096], BF16))
    p2f = es4.enter_context(nc.sbuf_tensor("s4_p2f", [128, 4096], F32))
    lam2 = es4.enter_context(nc.sbuf_tensor("s4_lam2", [128, 64], F32))
    xs = es4.enter_context(nc.sbuf_tensor("s4_xs", [128, 4096], BF16))
    sbf = es4.enter_context(nc.sbuf_tensor("s4_sb", [128, 65], F32))
    atb = es4.enter_context(nc.sbuf_tensor("s4_at", [128, 64], BF16))
    ybuf = es4.enter_context(nc.sbuf_tensor("s4_yb", [128, 4096], F32))
    ps_s = es4.enter_context(nc.psum_tensor("s4_ps_s", [128, 512], F32))
    ps_y = es4.enter_context(nc.psum_tensor("s4_ps_y", [128, 512], F32))
    ps_t = es4.enter_context(nc.psum_tensor("s4_ps_t", [128, 128], F32))
    if True:
        nc.sync.dma_start(out=idt[:], in_=ident).then_inc(sl.h, 16)
        sl.inc(16)
        for dst_, src_ in [(cArow, Arow), (cInvA, invA), (cKvals, kvals),
                           (cM63, m63col), (cLp1, lp1c)]:
            nc.sync.dma_start(out=dst_[:], in_=src_).then_inc(sl.h, 16)
            sl.inc(16)
        nc.vector.memset(sbf[:, 0:1], 0.0)
        nc.vector.drain().then_inc(svv.h, 1)
        svv.inc(1)

        for ch in range(8):
            nc.sync.dma_start(out=ldt[:], in_=logdt.ap()[ch, :].unsqueeze(1)).then_inc(sl.h, 16)
            sl.inc(16)
            nc.sync.dma_start(out=bc2[:, 0:64], in_=Bm.ap()[ch]).then_inc(sl.h, 16)
            sl.inc(16)
            nc.sync.dma_start(out=bc2[:, 64:128], in_=Cm.ap()[ch]).then_inc(sl.h, 16)
            sl.inc(16)
            nc.sync.dma_start(out=xs[0:64, :],
                              in_=AP(xg, ch * 128, [[E, 64], [64 * E, 64], [1, 64]])).then_inc(sl.h, 16)
            sl.inc(16)
            nc.sync.dma_start(out=xs[64:128, :],
                              in_=AP(xg, ch * 128 + 64, [[E, 64], [64 * E, 64], [1, 64]])).then_inc(sl.h, 16)
            sl.inc(16)
            nc.scalar.wait_ge(sl.h, sl.n)
            nc.scalar.activation(dtc[:], ldt[:], AF.Exp)
            nc.scalar.drain().then_inc(sc.h, 1)
            sc.inc(1)
            nc.vector.wait_ge(sc.h, sc.n)
            nc.vector.wait_ge(sl.h, sl.n)
            nc.vector.tensor_mul(dta[:], cArow[:], dtc[:, 0:1].to_broadcast([128, 64]))
            nc.vector.drain().then_inc(svv.h, 1)
            svv.inc(1)
            nc.scalar.wait_ge(svv.h, svv.n)
            nc.scalar.activation(wv[:], dta[:], AF.Exp)
            nc.scalar.activation(laml[:], dta[:], AF.Exp, scale=64.0)
            nc.scalar.drain().then_inc(sc.h, 1)
            sc.inc(1)
            nc.vector.wait_ge(sc.h, sc.n)
            nc.vector.tensor_scalar(wv[:], wv[:], -1.0, None, ALU.add)
            nc.vector.tensor_mul(wv[:], wv[:], cInvA[:])
            nc.vector.tensor_mul(wv[:], wv[:], bc2[:, 0:64])
            nc.vector.tensor_mul(wv[:], wv[:], bc2[:, 64:128])
            nc.vector.drain().then_inc(svv.h, 1)
            svv.inc(1)
            nc.gpsimd.wait_ge(svv.h, svv.n)
            nc.gpsimd.dma_start(out=dtA_d.ap(), in_=dta[:]).then_inc(sb.h, 16)
            sb.inc(16)
            nc.gpsimd.dma_start(out=w_d.ap(), in_=wv[:]).then_inc(sb.h, 16)
            sb.inc(16)
            nc.tensor.wait_ge(svv.h, svv.n)
            nc.tensor.wait_ge(sl.h, sl.n)
            nc.tensor.transpose(ps_t[0:64, :], dta[:], idt[:]).then_inc(smm.h, 1)
            smm.inc(1)
            nc.vector.wait_ge(smm.h, smm.n)
            nc.vector.tensor_copy(big2[0:64, 0:128], ps_t[0:64, :])
            nc.vector.drain().then_inc(svv.h, 1)
            svv.inc(1)
            nc.tensor.wait_ge(svv.h, svv.n)
            nc.tensor.transpose(ps_t[0:64, :], laml[:], idt[:]).then_inc(smm.h, 1)
            smm.inc(1)
            nc.vector.wait_ge(smm.h, smm.n)
            nc.vector.tensor_copy(big2[0:64, 128:256], ps_t[0:64, :])
            nc.vector.drain().then_inc(svv.h, 1)
            svv.inc(1)
            nc.gpsimd.wait_ge(svv.h, svv.n)
            nc.gpsimd.dma_start(out=dtAT_d.ap(), in_=big2[0:64, 0:128]).then_inc(sb.h, 16)
            sb.inc(16)
            nc.gpsimd.dma_start(out=lamLT_d.ap(), in_=big2[0:64, 128:256]).then_inc(sb.h, 16)
            sb.inc(16)
            nc.sync.wait_ge(sb.h, sb.n)
            nc.sync.dma_start(out=big[:], in_=AP(dtA_d, 0, [[64, 128], [0, 64], [1, 64]])).then_inc(sl.h, 16)
            sl.inc(16)
            nc.sync.dma_start(out=big2[:], in_=AP(w_d, 0, [[64, 128], [0, 64], [1, 64]])).then_inc(sl.h, 16)
            sl.inc(16)
            nc.vector.wait_ge(sl.h, sl.n)
            nc.vector.tensor_mul(big[:], big[:], cKvals[:])
            nc.vector.drain().then_inc(svv.h, 1)
            svv.inc(1)
            nc.scalar.wait_ge(svv.h, svv.n)
            nc.scalar.activation(big[:], big[:], AF.Exp)
            nc.scalar.drain().then_inc(sc.h, 1)
            sc.inc(1)
            nc.vector.wait_ge(sc.h, sc.n)
            nc.vector.tensor_mul(big[:], big[:], big2[:])
            nc.vector.memset(tpad[:, 0:64], 0.0)
            nc.vector.reduce_sum(tpad[:, 64:128],
                                 AP(big, 0, [[4096, 128], [64, 64], [1, 64]]),
                                 axis=AX.X)
            nc.vector.drain().then_inc(svv.h, 1)
            svv.inc(1)
            nc.sync.wait_ge(svv.h, svv.n)
            nc.sync.dma_start(out=AP(kdram, 0, [[192, 128], [192 * 128 + 1, 64], [1, 128]]),
                              in_=AP(tpad, 0, [[128, 128], [0, 64], [1, 128]])).then_inc(sbs.h, 16)
            sbs.inc(16)
            nc.gpsimd.wait_ge(sbs.h, sbs.n)
            nc.gpsimd.wait_ge(sb.h, sb.n)
            nc.gpsimd.dma_start(out=ktb[0:64, :],
                                in_=AP(kdram, 64, [[192 * 128, 64], [192, 64], [1, 64]])).then_inc(sb.h, 16)
            sb.inc(16)
            nc.gpsimd.dma_start(out=ktb[64:128, :],
                                in_=AP(kdram, 64 * 192 + 64, [[192 * 128, 64], [192, 64], [1, 64]])).then_inc(sb.h, 16)
            sb.inc(16)
            nc.sync.dma_start(out=big[0:64, :],
                              in_=AP(dtAT_d, 0, [[0, 64], [128, 64], [1, 64]])).then_inc(sl.h, 16)
            sl.inc(16)
            nc.sync.dma_start(out=big[64:128, :],
                              in_=AP(dtAT_d, 64, [[0, 64], [128, 64], [1, 64]])).then_inc(sl.h, 16)
            sl.inc(16)
            nc.vector.wait_ge(sl.h, sl.n)
            nc.vector.tensor_scalar(big[:], big[:], cM63[:, 0:1], None, ALU.mult)
            nc.vector.drain().then_inc(svv.h, 1)
            svv.inc(1)
            nc.scalar.wait_ge(svv.h, svv.n)
            nc.scalar.activation(p1b[:], big[:], AF.Exp)
            nc.scalar.drain().then_inc(sc.h, 1)
            sc.inc(1)
            nc.sync.wait_ge(sc.h, sc.n)
            nc.sync.dma_start(out=big[0:64, :],
                              in_=AP(dtAT_d, 0, [[128, 64], [0, 64], [1, 64]])).then_inc(sl.h, 16)
            sl.inc(16)
            nc.sync.dma_start(out=big[64:128, :],
                              in_=AP(dtAT_d, 64, [[128, 64], [0, 64], [1, 64]])).then_inc(sl.h, 16)
            sl.inc(16)
            nc.vector.wait_ge(sl.h, sl.n)
            nc.vector.tensor_mul(big[:], big[:], cLp1[:])
            nc.vector.drain().then_inc(svv.h, 1)
            svv.inc(1)
            nc.scalar.wait_ge(svv.h, svv.n)
            nc.scalar.activation(p2f[:], big[:], AF.Exp)
            nc.scalar.drain().then_inc(sc.h, 1)
            sc.inc(1)
            nc.tensor.wait_ge(svv.h, svv.n)
            nc.tensor.transpose(ps_t[0:64, :], wv[:], idt[:]).then_inc(smm.h, 1)
            smm.inc(1)
            nc.vector.wait_ge(smm.h, smm.n)
            nc.vector.tensor_copy(big2[0:64, 0:128], ps_t[0:64, :])
            nc.vector.drain().then_inc(svv.h, 1)
            svv.inc(1)
            nc.gpsimd.wait_ge(svv.h, svv.n)
            nc.gpsimd.dma_start(out=wT_d.ap(), in_=big2[0:64, 0:128]).then_inc(sb.h, 16)
            sb.inc(16)
            nc.sync.wait_ge(sb.h, sb.n)
            nc.sync.wait_ge(sc.h, sc.n)
            nc.sync.dma_start(out=big[0:64, :],
                              in_=AP(wT_d, 0, [[128, 64], [0, 64], [1, 64]])).then_inc(sl.h, 16)
            sl.inc(16)
            nc.sync.dma_start(out=big[64:128, :],
                              in_=AP(wT_d, 64, [[128, 64], [0, 64], [1, 64]])).then_inc(sl.h, 16)
            sl.inc(16)
            nc.vector.wait_ge(sl.h, sl.n)
            nc.vector.wait_ge(sc.h, sc.n)
            nc.vector.tensor_mul(p2f[:], p2f[:], big[:])
            nc.vector.tensor_copy(p2b[:], p2f[:])
            nc.sync.dma_start(out=lam2[0:64, :],
                              in_=AP(lamLT_d, 0, [[128, 64], [1, 64]])).then_inc(sl.h, 16)
            sl.inc(16)
            nc.sync.dma_start(out=lam2[64:128, :],
                              in_=AP(lamLT_d, 64, [[128, 64], [1, 64]])).then_inc(sl.h, 16)
            sl.inc(16)
            nc.vector.drain().then_inc(svv.h, 1)
            svv.inc(1)
            nc.tensor.wait_ge(svv.h, svv.n)
            nc.tensor.wait_ge(sc.h, sc.n)
            nc.tensor.wait_ge(sb.h, sb.n)
            nc.vector.wait_ge(sl.h, sl.n)

            for j in range(64):
                sslot = (j % 8) * 64
                yslot = (j % 8) * 64
                nc.tensor.matmul(ps_s[0:64, sslot:sslot + 64],
                                 p1b[0:64, j:4096:64], xs[0:64, j:4096:64],
                                 start=True, stop=True, tile_position=(0, 0))
                nc.tensor.matmul(ps_s[64:128, sslot:sslot + 64],
                                 p1b[64:128, j:4096:64], xs[64:128, j:4096:64],
                                 start=True, stop=True, tile_position=(64, 64)).then_inc(smm.h, 1)
                smm.inc(1)
                nc.vector.wait_ge(smm.h, smm.n)
                nc.vector.tensor_copy(sbf[:, 1:65], ps_s[:, sslot:sslot + 64])
                nc.vector.drain().then_inc(svv.h, 1)
                svv.inc(1)
                nc.vector.wait_ge(svv.h, svv.n)
                nc.vector.tensor_tensor_scan(atf[:],
                                             lam2[:, j:j + 1].to_broadcast([128, 64]),
                                             sbf[:, 0:64], 0.0, ALU.mult, ALU.add)
                nc.vector.drain().then_inc(svv.h, 1)
                svv.inc(1)
                nc.vector.wait_ge(svv.h, svv.n)
                nc.vector.tensor_copy(atb[:], atf[:])
                nc.vector.drain().then_inc(svv.h, 1)
                svv.inc(1)
                if dbg and ch == 7 and j == 0:
                    for nm, tile in [("d_atf0", atf[:]), ("d_sbf0", sbf[:]), ("d_lam2c", lam2[:]), ("d_atb0", atb[:])]:
                        dd = cx.dt(nm, list(tile.shape), tile.dtype, kind="ExternalOutput")
                        nc.gpsimd.wait_ge(svv.h, svv.n)
                        nc.gpsimd.dma_start(out=dd.ap(), in_=tile).then_inc(sst.h, 16)
                        sst.inc(16)
                    nc.vector.wait_ge(sst.h, sst.n)
                nc.tensor.wait_ge(svv.h, svv.n)
                nc.tensor.matmul(ps_y[0:64, yslot:yslot + 64],
                                 xs[0:64, j:4096:64], ktb[0:64, j * 64:(j + 1) * 64],
                                 start=True, stop=False, tile_position=(0, 0))
                nc.tensor.matmul(ps_y[0:64, yslot:yslot + 64],
                                 atb[0:64, :], p2b[0:64, j:4096:64],
                                 start=False, stop=True, tile_position=(0, 0))
                nc.tensor.matmul(ps_y[64:128, yslot:yslot + 64],
                                 xs[64:128, j:4096:64], ktb[64:128, j * 64:(j + 1) * 64],
                                 start=True, stop=False, tile_position=(64, 64))
                nc.tensor.matmul(ps_y[64:128, yslot:yslot + 64],
                                 atb[64:128, :], p2b[64:128, j:4096:64],
                                 start=False, stop=True, tile_position=(64, 64)).then_inc(smm.h, 1)
                smm.inc(1)
                nc.vector.wait_ge(smm.h, smm.n)
                nc.vector.tensor_copy(ybuf[:, j * 64:(j + 1) * 64], ps_y[:, yslot:yslot + 64])
                nc.vector.drain().then_inc(svv.h, 1)
                svv.inc(1)
                if serial:
                    nc.tensor.wait_ge(svv.h, svv.n)
                    nc.scalar.wait_ge(svv.h, svv.n)
            nc.gpsimd.wait_ge(svv.h, svv.n)
            nc.gpsimd.dma_start(
                out=AP(ys4f, ch * 128 * T, [[64, 64], [T, 64], [1, 64]]),
                in_=AP(ybuf, 0, [[4096, 64], [64, 64], [1, 64]])).then_inc(sst.h, 16)
            sst.inc(16)
            nc.gpsimd.dma_start(
                out=AP(ys4f, (ch * 128 + 64) * T, [[64, 64], [T, 64], [1, 64]]),
                in_=AP(ybuf, 64 * 4096, [[4096, 64], [64, 64], [1, 64]])).then_inc(sst.h, 16)
            sst.inc(16)
            if dbg and ch == 7:
                for nm, tile in [("d_tpad", tpad[:]), ("d_ktb", ktb[:]), ("d_p1b", p1b[:]),
                                 ("d_p2b", p2b[:]), ("d_lam2", lam2[:]), ("d_sbf", sbf[:]),
                                 ("d_atf", atf[:]), ("d_ybuf", ybuf[:]), ("d_xs", xs[:])]:
                    dd = cx.dt(nm, list(tile.shape), tile.dtype, kind="ExternalOutput")
                    nc.gpsimd.dma_start(out=dd.ap(), in_=tile).then_inc(sst.h, 16)
                    sst.inc(16)
            nc.sync.wait_ge(sst.h, sst.n)
            nc.vector.wait_ge(sst.h, sst.n)
            nc.tensor.wait_ge(sst.h, sst.n)
            nc.scalar.wait_ge(sst.h, sst.n)
            nc.gpsimd.wait_ge(sst.h, sst.n)
        nc.gpsimd.wait_ge(sst.h, sst.n)
    es4.close()
    return ys4f


"""xn-prep and combine builders.

build_xnprep: xr f32 [8,128,T] -> xnf f32 + xnb bf16 [8,128,T]   (plain LN per token)
build_combine: x f32 + attn f32 + ffn f32 ([8,128,2048]) -> x1 f32 + xn1b bf16
Both use the verified centered-LN pattern (cross-engine stats only).
xcols input [128,2]: col0 = 1e-5 (eps bias AP).
"""
import numpy as np
from contextlib import ExitStack
from concourse.ap import AP


def _ln_tile(nc, cx, sems, bufs, ntok_chunks, xcols, with_ln=True):
    pass


def build_xnprep(cx, xr, xnf, xnb, xcols_d, ntiles=32, pfx="xp"):
    nc = cx.nc
    sl = cx.sem(f"{pfx}_l")
    sm = cx.sem(f"{pfx}_m")
    ss = cx.sem(f"{pfx}_s")
    sv = cx.sem(f"{pfx}_v")
    sgo = cx.sem(f"{pfx}_go")
    es = ExitStack()
    S = lambda n, sh, dt_: es.enter_context(nc.sbuf_tensor(n, sh, dt_))
    xst = [S(f"{pfx}_xst{i}", [128, 1024], F32) for i in range(2)]
    lnt = S(f"{pfx}_lnt", [128, 1024], F32)
    tmB = S(f"{pfx}_tmB", [128, 1024], F32)
    lncol = S(f"{pfx}_lncol", [128, 8], F32)
    xof = S(f"{pfx}_xof", [128, 1024], F32)
    xob = S(f"{pfx}_xob", [128, 1024], BF16)
    xcols = S(f"{pfx}_xcols", [128, 2], F32)
    idf = S(f"{pfx}_idf", [128, 128], F32)
    pstf = es.enter_context(nc.psum_tensor(f"{pfx}_ps", [128, 128], F32))
    nc.sync.dma_start(out=idf[:], in_=nc.inline_tensor(f32arr(np.eye(128, dtype=np.float32)), f"c_{pfx}idf").ap()).then_inc(sl.h, 16)
    sl.inc(16)
    nc.sync.dma_start(out=xcols[:], in_=xcols_d.ap()).then_inc(sl.h, 16)
    sl.inc(16)

    def load_tile(it):
        c0 = it * 128
        for kc in range(8):
            nc.sync.dma_start(out=xst[it % 2][:, kc * 128:(kc + 1) * 128],
                              in_=xr.ap()[kc, :, c0:c0 + 128]).then_inc(sl.h, 16)
            sl.inc(16)

    load_tile(0)
    for it in range(ntiles):
        c0 = it * 128
        if it + 1 < ntiles:
            nc.sync.wait_ge(sv.h, sv.n)
            load_tile(it + 1)
        xs = xst[it % 2]
        nc.tensor.wait_ge(sl.h, sl.n - (16 * 8 if it + 1 < ntiles else 0))
        for kc in range(8):
            nc.tensor.wait_ge(sv.h, sv.n)
            nc.tensor.transpose(pstf[:, :], xs[:, kc * 128:(kc + 1) * 128], idf[:]).then_inc(sm.h, 1)
            sm.inc(1)
            nc.vector.wait_ge(sm.h, sm.n)
            nc.vector.tensor_copy(lnt[:, kc * 128:(kc + 1) * 128], pstf[:, :])
            nc.vector.drain().then_inc(sv.h, 1)
            sv.inc(1)
        nc.vector.reduce_sum(lncol[:, 0:1], lnt[:, 0:1024], axis=AX.X)
        nc.vector.drain().then_inc(sv.h, 1)
        sv.inc(1)
        nc.scalar.wait_ge(sv.h, sv.n)
        nc.scalar.activation(lncol[:, 2:3], lncol[:, 0:1], AF.Identity, scale=1.0 / 1024.0)
        nc.scalar.drain().then_inc(ss.h, 1)
        ss.inc(1)
        nc.vector.wait_ge(ss.h, ss.n)
        nc.vector.tensor_sub(lnt[:, 0:1024], lnt[:, 0:1024],
                             lncol[:, 2:3].to_broadcast([128, 1024]))
        nc.vector.tensor_mul(tmB[:, 0:1024], lnt[:, 0:1024], lnt[:, 0:1024])
        nc.vector.reduce_sum(lncol[:, 1:2], tmB[:, 0:1024], axis=AX.X)
        nc.vector.drain().then_inc(sv.h, 1)
        sv.inc(1)
        nc.scalar.wait_ge(sv.h, sv.n)
        nc.scalar.activation(lncol[:, 6:7], lncol[:, 1:2], AF.Ln, scale=1.0 / 1024.0,
                             bias=xcols[:, 0:1])
        nc.scalar.drain().then_inc(ss.h, 1)
        ss.inc(1)
        nc.scalar.wait_ge(ss.h, ss.n)
        nc.scalar.activation(lncol[:, 7:8], lncol[:, 6:7], AF.Exp, scale=-0.5)
        nc.scalar.drain().then_inc(ss.h, 1)
        ss.inc(1)
        nc.vector.wait_ge(ss.h, ss.n)
        nc.vector.tensor_mul(lnt[:, 0:1024], lnt[:, 0:1024],
                             lncol[:, 7:8].to_broadcast([128, 1024]))
        nc.vector.drain().then_inc(sv.h, 1)
        sv.inc(1)
        for kc in range(8):
            nc.tensor.wait_ge(sv.h, sv.n)
            nc.tensor.wait_ge(sgo.h, sgo.n)
            nc.tensor.transpose(pstf[:, :], lnt[:, kc * 128:(kc + 1) * 128], idf[:]).then_inc(sm.h, 1)
            sm.inc(1)
            nc.vector.wait_ge(sm.h, sm.n)
            nc.vector.tensor_copy(xof[:, kc * 128:(kc + 1) * 128], pstf[:, :])
            nc.vector.tensor_copy(xob[:, kc * 128:(kc + 1) * 128], pstf[:, :])
            nc.vector.drain().then_inc(sv.h, 1)
            sv.inc(1)
            nc.tensor.wait_ge(sv.h, sv.n)
        nc.gpsimd.wait_ge(sv.h, sv.n)
        for kc in range(8):
            nc.gpsimd.dma_start(out=xnf.ap()[kc, :, c0:c0 + 128],
                                in_=xof[:, kc * 128:(kc + 1) * 128]).then_inc(sgo.h, 16)
            sgo.inc(16)
            nc.gpsimd.dma_start(out=xnb.ap()[kc, :, c0:c0 + 128],
                                in_=xob[:, kc * 128:(kc + 1) * 128]).then_inc(sgo.h, 16)
            sgo.inc(16)
    nc.sync.wait_ge(sgo.h, sgo.n)
    nc.vector.wait_ge(sgo.h, sgo.n)


def build_combine(cx, x_ring, attn_d, ffn_d, x1f, xn1b, xcols_d, pfx="cb"):
    nc = cx.nc
    sl = cx.sem(f"{pfx}_l")
    sm = cx.sem(f"{pfx}_m")
    ss = cx.sem(f"{pfx}_s")
    sv = cx.sem(f"{pfx}_v")
    sgo = cx.sem(f"{pfx}_go")
    es = ExitStack()
    S = lambda n, sh, dt_: es.enter_context(nc.sbuf_tensor(n, sh, dt_))
    xst = [S(f"{pfx}_xst{i}", [128, 1024], F32) for i in range(2)]
    ast = [S(f"{pfx}_ast{i}", [128, 1024], F32) for i in range(2)]
    fst = [S(f"{pfx}_fst{i}", [128, 1024], F32) for i in range(2)]
    x1s = S(f"{pfx}_x1s", [128, 1024], F32)
    lnt = S(f"{pfx}_lnt", [128, 1024], F32)
    tmB = S(f"{pfx}_tmB", [128, 1024], F32)
    lncol = S(f"{pfx}_lncol", [128, 8], F32)
    xob = S(f"{pfx}_xob", [128, 1024], BF16)
    xcols = S(f"{pfx}_xcols", [128, 2], F32)
    idf = S(f"{pfx}_idf", [128, 128], F32)
    pstf = es.enter_context(nc.psum_tensor(f"{pfx}_ps", [128, 128], F32))
    nc.sync.dma_start(out=idf[:], in_=nc.inline_tensor(f32arr(np.eye(128, dtype=np.float32)), f"c_{pfx}idf").ap()).then_inc(sl.h, 16)
    sl.inc(16)
    nc.sync.dma_start(out=xcols[:], in_=xcols_d.ap()).then_inc(sl.h, 16)
    sl.inc(16)

    def load_tile(it):
        c0 = it * 128
        for kc in range(8):
            nc.sync.dma_start(out=xst[it % 2][:, kc * 128:(kc + 1) * 128],
                              in_=x_ring.ap()[kc, :, c0:c0 + 128]).then_inc(sl.h, 16)
            sl.inc(16)
            nc.sync.dma_start(out=ast[it % 2][:, kc * 128:(kc + 1) * 128],
                              in_=attn_d.ap()[kc, :, c0:c0 + 128]).then_inc(sl.h, 16)
            sl.inc(16)
            nc.sync.dma_start(out=fst[it % 2][:, kc * 128:(kc + 1) * 128],
                              in_=ffn_d.ap()[kc, :, c0:c0 + 128]).then_inc(sl.h, 16)
            sl.inc(16)

    load_tile(0)
    for it in range(16):
        c0 = it * 128
        nc.vector.wait_ge(sl.h, sl.n)
        if it + 1 < 16:
            nc.sync.wait_ge(sv.h, sv.n)
            load_tile(it + 1)
        p = it % 2
        nc.vector.wait_ge(sgo.h, sgo.n)
        nc.vector.tensor_add(x1s[:, 0:1024], xst[p][:, 0:1024], ast[p][:, 0:1024])
        nc.vector.tensor_add(x1s[:, 0:1024], x1s[:, 0:1024], fst[p][:, 0:1024])
        nc.vector.drain().then_inc(sv.h, 1)
        sv.inc(1)
        nc.gpsimd.wait_ge(sv.h, sv.n)
        nc.gpsimd.wait_ge(sgo.h, sgo.n)
        for kc in range(8):
            nc.gpsimd.dma_start(out=x1f.ap()[kc, :, c0:c0 + 128],
                                in_=x1s[:, kc * 128:(kc + 1) * 128]).then_inc(sgo.h, 16)
            sgo.inc(16)
        for kc in range(8):
            nc.tensor.wait_ge(sv.h, sv.n)
            nc.tensor.transpose(pstf[:, :], x1s[:, kc * 128:(kc + 1) * 128], idf[:]).then_inc(sm.h, 1)
            sm.inc(1)
            nc.vector.wait_ge(sm.h, sm.n)
            nc.vector.tensor_copy(lnt[:, kc * 128:(kc + 1) * 128], pstf[:, :])
            nc.vector.drain().then_inc(sv.h, 1)
            sv.inc(1)
        nc.vector.reduce_sum(lncol[:, 0:1], lnt[:, 0:1024], axis=AX.X)
        nc.vector.drain().then_inc(sv.h, 1)
        sv.inc(1)
        nc.scalar.wait_ge(sv.h, sv.n)
        nc.scalar.activation(lncol[:, 2:3], lncol[:, 0:1], AF.Identity, scale=1.0 / 1024.0)
        nc.scalar.drain().then_inc(ss.h, 1)
        ss.inc(1)
        nc.vector.wait_ge(ss.h, ss.n)
        nc.vector.tensor_sub(lnt[:, 0:1024], lnt[:, 0:1024],
                             lncol[:, 2:3].to_broadcast([128, 1024]))
        nc.vector.tensor_mul(tmB[:, 0:1024], lnt[:, 0:1024], lnt[:, 0:1024])
        nc.vector.reduce_sum(lncol[:, 1:2], tmB[:, 0:1024], axis=AX.X)
        nc.vector.drain().then_inc(sv.h, 1)
        sv.inc(1)
        nc.scalar.wait_ge(sv.h, sv.n)
        nc.scalar.activation(lncol[:, 6:7], lncol[:, 1:2], AF.Ln, scale=1.0 / 1024.0,
                             bias=xcols[:, 0:1])
        nc.scalar.drain().then_inc(ss.h, 1)
        ss.inc(1)
        nc.scalar.wait_ge(ss.h, ss.n)
        nc.scalar.activation(lncol[:, 7:8], lncol[:, 6:7], AF.Exp, scale=-0.5)
        nc.scalar.drain().then_inc(ss.h, 1)
        ss.inc(1)
        nc.vector.wait_ge(ss.h, ss.n)
        nc.vector.tensor_mul(lnt[:, 0:1024], lnt[:, 0:1024],
                             lncol[:, 7:8].to_broadcast([128, 1024]))
        nc.vector.drain().then_inc(sv.h, 1)
        sv.inc(1)
        for kc in range(8):
            nc.tensor.wait_ge(sv.h, sv.n)
            nc.tensor.wait_ge(sgo.h, sgo.n)
            nc.tensor.transpose(pstf[:, :], lnt[:, kc * 128:(kc + 1) * 128], idf[:]).then_inc(sm.h, 1)
            sm.inc(1)
            nc.vector.wait_ge(sm.h, sm.n)
            nc.vector.tensor_copy(xob[:, kc * 128:(kc + 1) * 128], pstf[:, :])
            nc.vector.drain().then_inc(sv.h, 1)
            sv.inc(1)
            nc.tensor.wait_ge(sv.h, sv.n)
        nc.gpsimd.wait_ge(sv.h, sv.n)
        for kc in range(8):
            nc.gpsimd.dma_start(out=xn1b.ap()[kc, :, c0:c0 + 128],
                                in_=xob[:, kc * 128:(kc + 1) * 128]).then_inc(sgo.h, 16)
            sgo.inc(16)
    nc.sync.wait_ge(sgo.h, sgo.n)
    nc.vector.wait_ge(sgo.h, sgo.n)


"""L0_ffn layer builder (local-attention mini-transformer layer).

See module docstring history in git... Host inputs per layer (nm dict):
  wqkv [1024,1536] f32 ; wo [512,1024] f32
  w1p  [1024,5632] f32 cols [a0|g0|a1|g1|...] 128-blocks (zero-padded halves)
  w2p  [2816,1024] f32 rows in a-block order
  cols [128,64] f32: 0-43 b1p per w1p-out-chunk, 44-51 b2, 52-59 lng,
       60 qs2, 61 ks2, 62 fshift
  lnb  [128,8] f32
  rot  [4,4096,128] bf16 TOKEN-major xpos tables (q-cos,q-sin(signed),
       k-cos,k-sin(signed)); 128 d-cols = two 64-stacks
  mq   [17,128,1024] bf16 additive masks qt-major
  mt   [17,8,128,128] bf16 additive masks kt-major
Ring in: xr_b16/xr_f32 [8,128,4096]; out: of32/ob16 [8,128,4096] ([0,2048)).
"""
import numpy as np
import concourse.mybir as mybir
from concourse.ap import AP

W_LIST = [7, 0, 1, 2, 3]
QT_OF_W = {7: [31], 0: [0, 1, 2, 3], 1: [4, 5, 6, 7], 2: [8, 9, 10, 11], 3: [12, 13, 14, 15]}
QTILES = [31] + list(range(16))
QIDX = {qt: i for i, qt in enumerate(QTILES)}


def build_l0ffn_layer(cx, li, xr_b16, xr_f32, of32, ob16, nm, dbg=False):
    nc = cx.nc
    g = lambda n: cx.dram[nm[n]]
    wqkv, wo, w1p, w2p = g("wqkv"), g("wo"), g("w1p"), g("w2p")
    cols_d, lnb_d, rot_d, mq_d, mt_d = g("cols"), g("lnb"), g("rot"), g("mq"), g("mt")
    qkv_d = cx.dt(f"l0f_qkv{li}", [12, 128, T], BF16)
    xmid_f = cx.dt(f"l0f_xmid{li}", [8, 128, 2176], F32)
    xmid_b = cx.dt(f"l0f_xmidb{li}", [8, 128, 2176], BF16)

    sl = cx.sem(f"f{li}_load")
    sg = cx.sem(f"f{li}_gload")
    sm = cx.sem(f"f{li}_mm")
    ss = cx.sem(f"f{li}_sc")
    sv = cx.sem(f"f{li}_vec")
    sso = cx.sem(f"f{li}_sto")
    sgo = cx.sem(f"f{li}_gsto")

    from contextlib import ExitStack
    es = ExitStack()
    S = lambda n, sh, dt_: es.enter_context(nc.sbuf_tensor(n, sh, dt_))
    Pf = lambda n, sh: es.enter_context(nc.psum_tensor(n, sh, F32))
    Pb = lambda n, sh: es.enter_context(nc.psum_tensor(n, sh, BF16))

    wq_s = S(f"f{li}_wqkv", [128, 8 * 1536], BF16)
    wo_s = S(f"f{li}_wo", [128, 4 * 1024], BF16)
    xin = S(f"f{li}_xin", [128, 8 * 512], BF16)
    qkvo = S(f"f{li}_qkvo", [128, 2 * 512], BF16)
    cols = S(f"f{li}_cols", [128, 66], F32)
    lnbc = S(f"f{li}_lnb", [128, 8], F32)
    idb = S(f"f{li}_idb", [128, 128], BF16)
    idf = S(f"f{li}_idf", [128, 128], F32)
    onesb = S(f"f{li}_ones", [128, 4], BF16)
    onesrow = S(f"f{li}_onesr", [1, 128], BF16)
    qsbc = S(f"f{li}_qsbc", [128, 256], F32)
    qraw = S(f"f{li}_qraw", [128, 512], BF16)
    kraw = S(f"f{li}_kraw", [128, 1024], BF16)
    vraw = S(f"f{li}_vraw", [128, 1024], BF16)
    rq = S(f"f{li}_rq", [128, 2 * 512], BF16)
    rk = S(f"f{li}_rk", [128, 2 * 1024], BF16)
    tmA = S(f"f{li}_tmA", [128, 1024], F32)
    tmB = S(f"f{li}_tmB", [128, 1024], F32)
    tmC = S(f"f{li}_tmC", [128, 1024], F32)
    ncol = S(f"f{li}_ncol", [128, 8], F32)
    qf = S(f"f{li}_qf", [128, 512], BF16)
    kff = S(f"f{li}_kff", [128, 1024], BF16)
    kf = S(f"f{li}_kf", [66, 2 * 1024], BF16)
    qa = S(f"f{li}_qa", [66, 2 * 512], BF16)
    vt = S(f"f{li}_vt", [128, 8 * 128], BF16)
    mqs = S(f"f{li}_mqs", [128, 1024], BF16)
    mts = S(f"f{li}_mts", [128, 8 * 128], BF16)
    simb = S(f"f{li}_simb", [128, 2 * 1024], F32)
    statc = S(f"f{li}_stat", [128, 16], F32)
    attnT = S(f"f{li}_attnT", [128, 8 * 512], BF16)
    oconc = S(f"f{li}_oconc", [128, 4 * 512], BF16)
    xmw = S(f"f{li}_xmw", [128, 2 * 512], F32)
    lnt = S(f"f{li}_lnt", [128, 1024], F32)
    lncol = S(f"f{li}_lncol", [128, 8], F32)
    hln = S(f"f{li}_hln", [128, 8 * 1024], BF16)
    wblk = S(f"f{li}_wblk", [128, 8 * 256], BF16)
    gelt = S(f"f{li}_gel", [128, 1024], BF16)
    amid = S(f"f{li}_amid", [128, 1024], BF16)
    hbf = S(f"f{li}_hbf", [128, 22 * 1024], BF16)
    w2blk = S(f"f{li}_w2blk", [128, 2 * 1024], BF16)
    fino = S(f"f{li}_fin", [128, 2 * 1024], F32)

    ps_a = Pf(f"f{li}_psa", [128, 1024])
    ps_b = Pf(f"f{li}_psb", [128, 1024])
    ps_c = Pf(f"f{li}_psc", [128, 512])
    ps_d = Pf(f"f{li}_psd", [128, 512])
    ps_tb = Pb(f"f{li}_pstb", [128, 128])
    pstf = Pf(f"f{li}_pstf", [128, 128])

    nc.sync.dma_start(out=idb[:], in_=nc.inline_tensor(bfarr(np.eye(128, dtype=np.float32)), f"c_idb{li}").ap()).then_inc(sl.h, 16)
    sl.inc(16)
    nc.sync.dma_start(out=idf[:], in_=nc.inline_tensor(f32arr(np.eye(128, dtype=np.float32)), f"c_idf{li}").ap()).then_inc(sl.h, 16)
    sl.inc(16)
    nc.sync.dma_start(out=cols[:], in_=cols_d.ap()).then_inc(sl.h, 16)
    sl.inc(16)
    nc.sync.dma_start(out=lnbc[:], in_=lnb_d.ap()).then_inc(sl.h, 16)
    sl.inc(16)
    for kc in range(8):
        nc.gpsimd.dma_start(out=wq_s[:, kc * 1536:(kc + 1) * 1536],
                            in_=wqkv.ap()[kc * 128:(kc + 1) * 128, :]).then_inc(sg.h, 16)
        sg.inc(16)
    for kc in range(4):
        nc.gpsimd.dma_start(out=wo_s[:, kc * 1024:(kc + 1) * 1024],
                            in_=wo.ap()[kc * 128:(kc + 1) * 128, :]).then_inc(sg.h, 16)
        sg.inc(16)
    nc.scalar.wait_ge(sl.h, sl.n)
    nc.vector.wait_ge(sl.h, sl.n)
    nc.vector.memset(kf[64:66, :], 1.0)
    nc.vector.memset(onesb[:], 1.0)
    nc.vector.memset(onesrow[:], 1.0)
    if dbg:
        nc.vector.memset(lncol[:], 0.0)
        nc.vector.memset(qf[:], 0.0)
        nc.vector.memset(kff[:], 0.0)
        nc.vector.memset(qraw[:], 0.0)
        nc.vector.memset(rq[:], 0.0)
        nc.vector.memset(rk[:], 0.0)
    nc.vector.drain().then_inc(sv.h, 1)
    sv.inc(1)

    def col_to_bcast(col_ap, dst_ap):
        nc.tensor.wait_ge(sv.h, sv.n)
        nc.tensor.transpose(pstf[0:1, :], col_ap, idf[:]).then_inc(sm.h, 1)
        sm.inc(1)
        nc.vector.wait_ge(sm.h, sm.n)
        nc.vector.tensor_copy(tmA[0:1, 0:128], pstf[0:1, 0:128])
        nc.vector.tensor_copy(gelt[0:1, 0:128], pstf[0:1, 0:128])
        nc.vector.drain().then_inc(sv.h, 1)
        sv.inc(1)
        nc.tensor.wait_ge(sv.h, sv.n)
        nc.tensor.matmul(pstf[:, 0:128], onesrow[:],
                         gelt[0:1, 0:128], start=True, stop=True).then_inc(sm.h, 1)
        sm.inc(1)
        nc.vector.wait_ge(sm.h, sm.n)
        nc.vector.tensor_copy(dst_ap, pstf[:, 0:128])
        nc.vector.drain().then_inc(sv.h, 1)
        sv.inc(1)

    col_to_bcast(cols[:, 60:61], qsbc[:, 0:128])
    col_to_bcast(cols[:, 61:62], qsbc[:, 128:256])

    for tw in range(8):
        t0 = tw * 512
        nc.sync.wait_ge(sv.h, sv.n)
        for kc in range(8):
            if kc < 4:
                nc.sync.dma_start(out=xin[:, kc * 512:(kc + 1) * 512],
                                  in_=xr_b16.ap()[kc, :, t0:t0 + 512]).then_inc(sl.h, 16)
                sl.inc(16)
            elif t0 > 0:
                nc.sync.dma_start(out=xin[:, kc * 512:(kc + 1) * 512],
                                  in_=xr_b16.ap()[kc, :, t0 - 1:t0 + 511]).then_inc(sl.h, 16)
                sl.inc(16)
            else:
                nc.sync.dma_start(out=xin[:, kc * 512 + 1:(kc + 1) * 512],
                                  in_=xr_b16.ap()[kc, :, 0:511]).then_inc(sl.h, 16)
                sl.inc(16)
                nc.sync.dma_start(out=xin[:, kc * 512:kc * 512 + 1],
                                  in_=xr_b16.ap()[kc, :, T - 1:T],
                                  allow_slow_non_contiguous=True).then_inc(sl.h, 16)
                sl.inc(16)
        if tw == 0:
            nc.vector.wait_ge(sl.h, sl.n)
            for kc in range(4, 8):
                nc.vector.tensor_mul(xin[:, kc * 512:kc * 512 + 1],
                                     xin[:, kc * 512:kc * 512 + 1],
                                     cols[:, 62:63])
            nc.vector.drain().then_inc(sv.h, 1)
            sv.inc(1)
            nc.tensor.wait_ge(sv.h, sv.n)
        nc.tensor.wait_ge(sl.h, sl.n)
        nc.tensor.wait_ge(sg.h, sg.n)
        for oc in range(12):
            pu = [ps_a, ps_b][oc % 2]
            nc.tensor.wait_ge(ss.h, ss.n)
            for kc in range(8):
                mm = nc.tensor.matmul(pu[:, 0:512],
                                      wq_s[:, kc * 1536 + oc * 128:kc * 1536 + (oc + 1) * 128],
                                      xin[:, kc * 512:(kc + 1) * 512],
                                      start=kc == 0, stop=kc == 7)
            mm.then_inc(sm.h, 1)
            sm.inc(1)
            nc.scalar.wait_ge(sm.h, sm.n)
            nc.scalar.wait_ge(sso.h, sso.n)
            nc.scalar.activation(qkvo[:, (oc % 2) * 512:(oc % 2 + 1) * 512], pu[:, 0:512], AF.Identity)
            nc.scalar.drain().then_inc(ss.h, 1)
            ss.inc(1)
            nc.sync.wait_ge(ss.h, ss.n)
            nc.sync.dma_start(out=qkv_d.ap()[oc, :, t0:t0 + 512],
                              in_=qkvo[:, (oc % 2) * 512:(oc % 2 + 1) * 512]).then_inc(sso.h, 16)
            sso.inc(16)
        nc.tensor.drain().then_inc(sv.h, 1)
        sv.inc(1)

    def prep_qk(raw, width, rtab, sinoff, scbase, outb, postmul=1.0):
        ntile = width // 128
        for it in range(ntile):
            c0 = it * 128
            nc.tensor.wait_ge(sv.h, sv.n)
            nc.tensor.wait_ge(sl.h, sl.n)
            nc.tensor.transpose(ps_tb[:, :], raw[:, c0:c0 + 128], idb[:]).then_inc(sm.h, 1)
            sm.inc(1)
            nc.vector.wait_ge(sm.h, sm.n)
            nc.vector.tensor_copy(tmA[:, 0:128], ps_tb[:, :])
            nc.vector.tensor_mul(tmB[:, 0:128], tmA[:, 0:128], tmA[:, 0:128])
            nc.vector.reduce_sum(ncol[:, 0:1], tmB[:, 0:64], axis=AX.X)
            nc.vector.reduce_sum(ncol[:, 1:2], tmB[:, 64:128], axis=AX.X)
            nc.vector.drain().then_inc(sv.h, 1)
            sv.inc(1)
            nc.scalar.wait_ge(sv.h, sv.n)
            nc.scalar.activation(ncol[:, 2:4], ncol[:, 0:2], AF.Ln, bias=cols[:, 63:64])
            nc.scalar.drain().then_inc(ss.h, 1)
            ss.inc(1)
            nc.scalar.wait_ge(ss.h, ss.n)
            nc.scalar.activation(ncol[:, 4:6], ncol[:, 2:4], AF.Exp, scale=-0.5)
            nc.scalar.drain().then_inc(ss.h, 1)
            ss.inc(1)
            nc.vector.wait_ge(ss.h, ss.n)
            nc.vector.tensor_mul(tmA[:, 0:64], tmA[:, 0:64], ncol[:, 4:5].to_broadcast([128, 64]))
            nc.vector.tensor_mul(tmA[:, 64:128], tmA[:, 64:128], ncol[:, 5:6].to_broadcast([128, 64]))
            if postmul != 1.0:
                nc.vector.tensor_scalar_mul(tmA[:, 0:128], tmA[:, 0:128], postmul)
            nc.vector.tensor_mul(tmA[:, 0:128], tmA[:, 0:128], qsbc[:, scbase:scbase + 128])
            nc.vector.tensor_copy(tmC[:, 0:32], tmA[:, 32:64])
            nc.vector.tensor_copy(tmC[:, 32:64], tmA[:, 0:32])
            nc.vector.tensor_copy(tmC[:, 64:96], tmA[:, 96:128])
            nc.vector.tensor_copy(tmC[:, 96:128], tmA[:, 64:96])
            nc.vector.tensor_mul(tmA[:, 0:128], tmA[:, 0:128], rtab[:, c0:c0 + 128])
            nc.vector.tensor_mul(tmC[:, 0:128], tmC[:, 0:128], rtab[:, sinoff + c0:sinoff + c0 + 128])
            nc.vector.tensor_add(tmB[:, 0:128], tmA[:, 0:128], tmC[:, 0:128])
            nc.vector.tensor_copy(gelt[:, 0:128], tmB[:, 0:128])
            nc.vector.drain().then_inc(sv.h, 1)
            sv.inc(1)
            nc.tensor.wait_ge(sv.h, sv.n)
            nc.tensor.transpose(ps_tb[:, :], gelt[:, 0:128], idb[:]).then_inc(sm.h, 1)
            sm.inc(1)
            nc.vector.wait_ge(sm.h, sm.n)
            nc.vector.tensor_copy(outb[:, c0:c0 + 128], ps_tb[:, :])
            nc.vector.drain().then_inc(sv.h, 1)
            sv.inc(1)

    for wq_ in W_LIST:
        qts = QT_OF_W[wq_]
        nq = len(qts)
        qw = nq * 128
        q0 = qts[0] * 128
        kw0 = (wq_ - 1) % 8
        spans = [(7 * 512, 512), (0, 512)] if kw0 == 7 else [(kw0 * 512, 1024)]
        nc.sync.wait_ge(sv.h, sv.n)
        nc.sync.wait_ge(sm.h, sm.n)
        for i, qt in enumerate(qts):
            nc.sync.dma_start(out=rq[:, i * 128:(i + 1) * 128],
                              in_=rot_d.ap()[0, qt * 128:(qt + 1) * 128, :]).then_inc(sl.h, 16)
            sl.inc(16)
            nc.sync.dma_start(out=rq[:, 512 + i * 128:512 + (i + 1) * 128],
                              in_=rot_d.ap()[1, qt * 128:(qt + 1) * 128, :]).then_inc(sl.h, 16)
            sl.inc(16)
        off = 0
        for (s0, ln) in spans:
            for sub in range(ln // 128):
                nc.sync.dma_start(out=rk[:, off + sub * 128:off + (sub + 1) * 128],
                                  in_=rot_d.ap()[2, s0 + sub * 128:s0 + (sub + 1) * 128, :]).then_inc(sl.h, 16)
                sl.inc(16)
                nc.sync.dma_start(out=rk[:, 1024 + off + sub * 128:1024 + off + (sub + 1) * 128],
                                  in_=rot_d.ap()[3, s0 + sub * 128:s0 + (sub + 1) * 128, :]).then_inc(sl.h, 16)
                sl.inc(16)
            off += ln
        for hp in range(4):
            nc.sync.wait_ge(sv.h, sv.n)
            nc.sync.wait_ge(sm.h, sm.n)
            nc.sync.dma_start(out=qraw[:, 0:qw], in_=qkv_d.ap()[hp, :, q0:q0 + qw]).then_inc(sl.h, 16)
            sl.inc(16)
            off = 0
            for (s0, ln) in spans:
                nc.sync.dma_start(out=kraw[:, off:off + ln],
                                  in_=qkv_d.ap()[4 + hp, :, s0:s0 + ln]).then_inc(sl.h, 16)
                sl.inc(16)
                nc.sync.dma_start(out=vraw[:, off:off + ln],
                                  in_=qkv_d.ap()[8 + hp, :, s0:s0 + ln]).then_inc(sl.h, 16)
                sl.inc(16)
                off += ln
            nc.vector.wait_ge(sl.h, sl.n)
            prep_qk(qraw, qw, rq, 512, 0, qf, postmul=8.0)
            prep_qk(kraw, 1024, rk, 1024, 128, kff)
            if dbg and wq_ == 7 and hp == 0:
                for nmt, tile in [("d_qsbc", qsbc[:]), ("d_qf", qf[:]), ("d_kff", kff[:]),
                                  ("d_qraw", qraw[:]), ("d_rq", rq[:]), ("d_rk", rk[:])]:
                    dd3 = cx.dt(nmt + str(li), list(tile.shape), tile.dtype, kind="ExternalOutput")
                    nc.gpsimd.wait_ge(sv.h, sv.n)
                    nc.gpsimd.dma_start(out=dd3.ap(), in_=tile).then_inc(sgo.h, 16)
                    sgo.inc(16)
                nc.vector.wait_ge(sgo.h, sgo.n)
                nc.sync.wait_ge(sgo.h, sgo.n)
            nc.vector.wait_ge(sv.h, sv.n)
            nc.vector.tensor_copy(kf[0:64, 0:1024], kff[0:64, :])
            nc.vector.tensor_copy(kf[0:64, 1024:2048], kff[64:128, :])
            nc.vector.drain().then_inc(sv.h, 1)
            sv.inc(1)
            nc.tensor.wait_ge(sl.h, sl.n)
            nc.tensor.wait_ge(sv.h, sv.n)
            for c8 in range(8):
                nc.tensor.transpose(ps_tb[:, :], vraw[:, c8 * 128:(c8 + 1) * 128], idb[:]).then_inc(sm.h, 1)
                sm.inc(1)
                nc.vector.wait_ge(sm.h, sm.n)
                nc.vector.tensor_copy(vt[:, c8 * 128:(c8 + 1) * 128], ps_tb[:, :])
                nc.vector.drain().then_inc(sv.h, 1)
                sv.inc(1)
                nc.tensor.wait_ge(sv.h, sv.n)
            for i, qt in enumerate(qts):
                nc.sync.wait_ge(sv.h, sv.n)
                nc.sync.dma_start(out=mqs[:], in_=mq_d.ap()[QIDX[qt]]).then_inc(sl.h, 16)
                sl.inc(16)
                nc.tensor.wait_ge(sv.h, sv.n)
                for hh in range(2):
                    pu = [ps_a, ps_b][hh]
                    tp = (0, 0) if hh == 0 else (64, 0)
                    rbase = hh * 64
                    for c2 in range(2):
                        mm = nc.tensor.matmul(
                            pu[0:128, c2 * 512:(c2 + 1) * 512],
                            qf[rbase:rbase + 64, i * 128:(i + 1) * 128],
                            kff[rbase:rbase + 64, c2 * 512:(c2 + 1) * 512],
                            start=True, stop=True, tile_position=tp)
                mm.then_inc(sm.h, 1)
                sm.inc(1)
                nc.vector.wait_ge(sm.h, sm.n)
                nc.vector.wait_ge(sl.h, sl.n)
                for hh in range(2):
                    pu = [ps_a, ps_b][hh]
                    nc.vector.scalar_tensor_tensor(simb[:, hh * 1024:(hh + 1) * 1024],
                                                   pu[:, 0:1024], 1.0, mqs[:],
                                                   ALU.mult, ALU.add)
                nc.vector.reduce_max(statc[:, 2:3], simb[:, 0:1024], axis=AX.X, negate=True)
                nc.vector.reduce_max(statc[:, 3:4], simb[:, 1024:2048], axis=AX.X, negate=True)
                nc.vector.memset(statc[:, 4:6], 0.0)
                nc.vector.drain().then_inc(sv.h, 1)
                sv.inc(1)
                nc.scalar.wait_ge(sv.h, sv.n)
                for hh in range(2):
                    nc.scalar.activation(tmA[:, 0:1024], simb[:, hh * 1024:(hh + 1) * 1024],
                                         AF.Exp, bias=statc[:, 2 + hh:3 + hh],
                                         accum_out=statc[:, 4 + hh:5 + hh])
                nc.scalar.drain().then_inc(ss.h, 1)
                ss.inc(1)
                nc.scalar.wait_ge(ss.h, ss.n)
                nc.scalar.activation(statc[:, 6:8], statc[:, 4:6], AF.Ln)
                nc.scalar.drain().then_inc(ss.h, 1)
                ss.inc(1)
                nc.scalar.wait_ge(ss.h, ss.n)
                for hh in range(2):
                    nc.scalar.activation(statc[:, 8 + hh:9 + hh], statc[:, 6 + hh:7 + hh],
                                         AF.Identity, scale=-1.0,
                                         bias=statc[:, 2 + hh:3 + hh])
                nc.scalar.drain().then_inc(ss.h, 1)
                ss.inc(1)
                nc.vector.wait_ge(ss.h, ss.n)
                nc.vector.tensor_copy(gelt[:, 0:2], statc[:, 8:10])
                nc.vector.drain().then_inc(sv.h, 1)
                sv.inc(1)
                nc.scalar.wait_ge(sv.h, sv.n)
                nc.scalar.activation(statc[:, 10:11], gelt[:, 0:1], AF.Identity)
                nc.scalar.activation(statc[:, 12:13], gelt[:, 1:2], AF.Identity)
                nc.scalar.drain().then_inc(ss.h, 1)
                ss.inc(1)
                nc.vector.wait_ge(ss.h, ss.n)
                nc.vector.tensor_sub(statc[:, 11:12], statc[:, 8:9], statc[:, 10:11])
                nc.vector.tensor_sub(statc[:, 13:14], statc[:, 9:10], statc[:, 12:13])
                nc.vector.drain().then_inc(sv.h, 1)
                sv.inc(1)
                for hh in range(2):
                    nc.tensor.wait_ge(sv.h, sv.n)
                    nc.tensor.transpose(pstf[0:2, :], statc[:, 10 + 2 * hh:12 + 2 * hh], idf[:]).then_inc(sm.h, 1)
                    sm.inc(1)
                    nc.vector.wait_ge(sm.h, sm.n)
                    nc.vector.tensor_copy(qa[64:66, hh * 512 + i * 128:hh * 512 + (i + 1) * 128],
                                          pstf[0:2, 0:128])
                    nc.vector.drain().then_inc(sv.h, 1)
                    sv.inc(1)
            nc.vector.wait_ge(sv.h, sv.n)
            nc.vector.tensor_copy(qa[0:64, 0:qw], qf[0:64, 0:qw])
            nc.vector.tensor_copy(qa[0:64, 512:512 + qw], qf[64:128, 0:qw])
            nc.vector.drain().then_inc(sv.h, 1)
            sv.inc(1)
            nc.tensor.wait_ge(sv.h, sv.n)
            for hh in range(2):
                for c8 in range(8):
                    nc.tensor.wait_ge(sv.h, sv.n)
                    mm = nc.tensor.matmul(ps_c[:, 0:qw],
                                          kf[0:66, hh * 1024 + c8 * 128:hh * 1024 + (c8 + 1) * 128],
                                          qa[0:66, hh * 512:hh * 512 + qw],
                                          start=True, stop=True)
                    mm.then_inc(sm.h, 1)
                    sm.inc(1)
                    if True:
                        nc.sync.wait_ge(sv.h, sv.n)
                        for i in range(nq):
                            nc.sync.dma_start(out=mts[:, i * 128:(i + 1) * 128],
                                              in_=mt_d.ap()[QIDX[qts[i]], c8]).then_inc(sl.h, 16)
                            sl.inc(16)
                    nc.vector.wait_ge(sm.h, sm.n)
                    nc.vector.wait_ge(sl.h, sl.n)
                    for i in range(nq):
                        nc.vector.scalar_tensor_tensor(
                            simb[:, i * 128:(i + 1) * 128],
                            ps_c[:, i * 128:(i + 1) * 128], 1.0,
                            mts[:, i * 128:(i + 1) * 128],
                            ALU.mult, ALU.add)
                    nc.vector.drain().then_inc(sv.h, 1)
                    sv.inc(1)
                    nc.scalar.wait_ge(sv.h, sv.n)
                    nc.scalar.activation(attnT[:, c8 * 512:c8 * 512 + qw], simb[:, 0:qw], AF.Exp)
                    nc.scalar.drain().then_inc(ss.h, 1)
                    ss.inc(1)
                tp = (0, 0) if hh == 0 else (0, 64)
                nc.tensor.wait_ge(ss.h, ss.n)
                for c8 in range(8):
                    mm = nc.tensor.matmul(ps_d[hh * 64:hh * 64 + 64, 0:qw],
                                          vt[:, c8 * 128 + hh * 64:c8 * 128 + hh * 64 + 64],
                                          attnT[:, c8 * 512:c8 * 512 + qw],
                                          start=c8 == 0, stop=c8 == 7, tile_position=tp)
                mm.then_inc(sm.h, 1)
                sm.inc(1)
            nc.vector.wait_ge(sm.h, sm.n)
            nc.vector.tensor_copy(oconc[:, hp * 512:hp * 512 + qw], ps_d[:, 0:qw])
            nc.vector.drain().then_inc(sv.h, 1)
            sv.inc(1)
        xbase = 0 if wq_ == 7 else 128 + wq_ * 512
        nc.tensor.wait_ge(sv.h, sv.n)
        for oc in range(8):
            pu = [ps_a, ps_b][oc % 2]
            nc.tensor.wait_ge(sv.h, sv.n)
            for kc in range(4):
                mm = nc.tensor.matmul(pu[:, 0:qw],
                                      wo_s[:, kc * 1024 + oc * 128:kc * 1024 + (oc + 1) * 128],
                                      oconc[:, kc * 512:kc * 512 + qw],
                                      start=kc == 0, stop=kc == 3)
            mm.then_inc(sm.h, 1)
            sm.inc(1)
            nc.sync.wait_ge(sv.h, sv.n)
            nc.sync.dma_start(out=xmw[:, 512:512 + qw],
                              in_=xr_f32.ap()[oc, :, q0:q0 + qw]).then_inc(sl.h, 16)
            sl.inc(16)
            nc.vector.wait_ge(sm.h, sm.n)
            nc.vector.wait_ge(sl.h, sl.n)
            nc.vector.scalar_tensor_tensor(xmw[:, 0:qw], pu[:, 0:qw], 1.0,
                                           xmw[:, 512:512 + qw], ALU.mult, ALU.add)
            nc.vector.tensor_copy(gelt[:, 0:qw], xmw[:, 0:qw])
            nc.vector.drain().then_inc(sv.h, 1)
            sv.inc(1)
            nc.gpsimd.wait_ge(sv.h, sv.n)
            nc.gpsimd.dma_start(out=xmid_f.ap()[oc, :, xbase:xbase + qw],
                                in_=xmw[:, 0:qw]).then_inc(sgo.h, 16)
            sgo.inc(16)
            nc.gpsimd.dma_start(out=xmid_b.ap()[oc, :, xbase:xbase + qw],
                                in_=gelt[:, 0:qw]).then_inc(sgo.h, 16)
            sgo.inc(16)
            nc.vector.wait_ge(sgo.h, sgo.n)

    for grp in range(2):
        gbase = 128 + grp * 1024
        nc.sync.wait_ge(sgo.h, sgo.n)
        nc.sync.wait_ge(sv.h, sv.n)
        for kc in range(8):
            base = gbase if kc < 4 else gbase - 1
            nc.sync.dma_start(out=hln[:, kc * 1024:(kc + 1) * 1024],
                              in_=xmid_b.ap()[kc, :, base:base + 1024]).then_inc(sl.h, 16)
            sl.inc(16)
        if grp == 0:
            nc.vector.wait_ge(sl.h, sl.n)
            for kc in range(4, 8):
                nc.vector.tensor_mul(hln[:, kc * 1024:kc * 1024 + 1],
                                     hln[:, kc * 1024:kc * 1024 + 1],
                                     cols[:, 62:63])
            nc.vector.drain().then_inc(sv.h, 1)
            sv.inc(1)
        for it in range(8):
            c0 = it * 128
            nc.tensor.wait_ge(sl.h, sl.n)
            nc.tensor.wait_ge(sv.h, sv.n)
            for kc in range(8):
                nc.tensor.transpose(ps_tb[:, :], hln[:, kc * 1024 + c0:kc * 1024 + c0 + 128], idb[:]).then_inc(sm.h, 1)
                sm.inc(1)
                nc.vector.wait_ge(sm.h, sm.n)
                nc.vector.tensor_copy(lnt[:, kc * 128:(kc + 1) * 128], ps_tb[:, :])
                nc.vector.drain().then_inc(sv.h, 1)
                sv.inc(1)
                nc.tensor.wait_ge(sv.h, sv.n)
            if dbg and grp == 0 and it == 0:
                dlnt = cx.dt(f"d_lnt{li}", [128, 1024], F32, kind="ExternalOutput")
                nc.sync.wait_ge(sv.h, sv.n)
                nc.sync.dma_start(out=dlnt.ap(), in_=lnt[:, 0:1024]).then_inc(sso.h, 16)
                sso.inc(16)
                nc.vector.wait_ge(sso.h, sso.n)
            nc.vector.reduce_sum(lncol[:, 0:1], lnt[:, 0:1024], axis=AX.X)
            nc.vector.drain().then_inc(sv.h, 1)
            sv.inc(1)
            nc.scalar.wait_ge(sv.h, sv.n)
            nc.scalar.activation(lncol[:, 2:3], lncol[:, 0:1], AF.Identity, scale=1.0 / 1024.0)
            nc.scalar.drain().then_inc(ss.h, 1)
            ss.inc(1)
            nc.vector.wait_ge(ss.h, ss.n)
            nc.vector.tensor_sub(lnt[:, 0:1024], lnt[:, 0:1024],
                                 lncol[:, 2:3].to_broadcast([128, 1024]))
            nc.vector.tensor_mul(tmB[:, 0:1024], lnt[:, 0:1024], lnt[:, 0:1024])
            nc.vector.reduce_sum(lncol[:, 1:2], tmB[:, 0:1024], axis=AX.X)
            nc.vector.drain().then_inc(sv.h, 1)
            sv.inc(1)
            nc.scalar.wait_ge(sv.h, sv.n)
            nc.scalar.activation(lncol[:, 6:7], lncol[:, 1:2], AF.Ln, scale=1.0 / 1024.0, bias=cols[:, 64:65])
            nc.scalar.drain().then_inc(ss.h, 1)
            ss.inc(1)
            nc.scalar.wait_ge(ss.h, ss.n)
            nc.scalar.activation(lncol[:, 7:8], lncol[:, 6:7], AF.Exp, scale=-0.5)
            nc.scalar.drain().then_inc(ss.h, 1)
            ss.inc(1)
            nc.vector.wait_ge(ss.h, ss.n)
            nc.vector.tensor_mul(lnt[:, 0:1024], lnt[:, 0:1024],
                                 lncol[:, 7:8].to_broadcast([128, 1024]))
            nc.vector.drain().then_inc(sv.h, 1)
            sv.inc(1)
            if dbg and grp == 0 and it == 0:
                dlnc = cx.dt(f"d_lncol{li}", [128, 8], F32, kind="ExternalOutput")
                nc.sync.wait_ge(sv.h, sv.n)
                nc.sync.dma_start(out=dlnc.ap(), in_=lncol[:, 0:8]).then_inc(sso.h, 16)
                sso.inc(16)
                nc.vector.wait_ge(sso.h, sso.n)
            for kc in range(8):
                nc.vector.wait_ge(sv.h, sv.n)
                nc.vector.wait_ge(sm.h, sm.n)
                nc.vector.tensor_copy(gelt[:, 0:128], lnt[:, kc * 128:(kc + 1) * 128])
                nc.vector.drain().then_inc(sv.h, 1)
                sv.inc(1)
                nc.tensor.wait_ge(sv.h, sv.n)
                nc.tensor.transpose(ps_tb[:, :], gelt[:, 0:128], idb[:]).then_inc(sm.h, 1)
                sm.inc(1)
                nc.scalar.wait_ge(sm.h, sm.n)
                nc.scalar.activation(hln[:, kc * 1024 + c0:kc * 1024 + c0 + 128], ps_tb[:, :],
                                     AF.Identity, scale=cols[:, 52 + kc:53 + kc],
                                     bias=lnbc[:, kc:kc + 1])
                nc.scalar.drain().then_inc(ss.h, 1)
                ss.inc(1)
                nc.tensor.wait_ge(ss.h, ss.n)
        if dbg and grp == 0:
            dd = cx.dt(f"d_hln{li}", [128, 8 * 1024], BF16, kind="ExternalOutput")
            nc.gpsimd.wait_ge(ss.h, ss.n)
            nc.gpsimd.dma_start(out=dd.ap(), in_=hln[:]).then_inc(sgo.h, 16)
            sgo.inc(16)
            nc.sync.wait_ge(sgo.h, sgo.n)
        for blk in range(22):
            nc.sync.wait_ge(sv.h, sv.n)
            nc.sync.wait_ge(sm.h, sm.n)
            for kc in range(8):
                nc.sync.dma_start(out=fino[:, kc * 256:(kc + 1) * 256],
                                  in_=w1p.ap()[kc * 128:(kc + 1) * 128, blk * 256:(blk + 1) * 256]).then_inc(sl.h, 16)
                sl.inc(16)
            nc.vector.wait_ge(sl.h, sl.n)
            for kc in range(8):
                nc.vector.tensor_copy(wblk[:, kc * 256:(kc + 1) * 256], fino[:, kc * 256:(kc + 1) * 256])
            nc.vector.drain().then_inc(sv.h, 1)
            sv.inc(1)
            nc.tensor.wait_ge(sv.h, sv.n)
            nc.tensor.wait_ge(ss.h, ss.n)
            nc.tensor.wait_ge(sv.h, sv.n)
            for half in range(2):
                pu = [ps_a, ps_b][half]
                for kc in range(8):
                    for th in range(2):
                        mm = nc.tensor.matmul(
                            pu[:, th * 512:(th + 1) * 512],
                            wblk[:, kc * 256 + half * 128:kc * 256 + half * 128 + 128],
                            hln[:, kc * 1024 + th * 512:kc * 1024 + (th + 1) * 512],
                            start=kc == 0, stop=kc == 7)
            mm.then_inc(sm.h, 1)
            sm.inc(1)
            nc.scalar.wait_ge(sm.h, sm.n)
            nc.scalar.wait_ge(sv.h, sv.n)
            nc.scalar.activation(amid[:, 0:1024], ps_a[:, 0:1024], AF.Identity,
                                 bias=cols[:, 2 * blk:2 * blk + 1])
            nc.scalar.activation(gelt[:, 0:1024], ps_b[:, 0:1024], AF.Gelu,
                                 bias=cols[:, 2 * blk + 1:2 * blk + 2])
            nc.scalar.drain().then_inc(ss.h, 1)
            ss.inc(1)
            nc.vector.wait_ge(ss.h, ss.n)
            nc.vector.tensor_mul(hbf[:, blk * 1024:(blk + 1) * 1024], amid[:, 0:1024], gelt[:, 0:1024])
            nc.vector.drain().then_inc(sv.h, 1)
            sv.inc(1)
        if dbg and grp == 0:
            dd2 = cx.dt(f"d_hbf{li}", [128, 22 * 1024], BF16, kind="ExternalOutput")
            nc.gpsimd.wait_ge(sv.h, sv.n)
            nc.gpsimd.dma_start(out=dd2.ap(), in_=hbf[:]).then_inc(sgo.h, 16)
            sgo.inc(16)
            nc.sync.wait_ge(sgo.h, sgo.n)
        for oc in range(8):
            pu = [ps_a, ps_b][oc % 2]
            nc.tensor.wait_ge(ss.h, ss.n)
            for kc in range(22):
                nc.sync.wait_ge(sm.h, sm.n) if kc == 0 else None
                nc.sync.wait_ge(sv.h, sv.n)
                nc.sync.dma_start(out=fino[:, (kc % 2) * 512:(kc % 2) * 512 + 128],
                                  in_=w2p.ap()[kc * 128:(kc + 1) * 128, oc * 128:(oc + 1) * 128]).then_inc(sl.h, 16)
                sl.inc(16)
                nc.vector.wait_ge(sl.h, sl.n)
                nc.vector.tensor_copy(w2blk[:, (kc % 2) * 1024:(kc % 2) * 1024 + 128],
                                      fino[:, (kc % 2) * 512:(kc % 2) * 512 + 128])
                nc.vector.drain().then_inc(sv.h, 1)
                sv.inc(1)
                nc.tensor.wait_ge(sv.h, sv.n)
                for th in range(2):
                    mm = nc.tensor.matmul(
                        pu[:, th * 512:(th + 1) * 512],
                        w2blk[:, (kc % 2) * 1024:(kc % 2) * 1024 + 128],
                        hbf[:, kc * 1024 + th * 512:kc * 1024 + (th + 1) * 512],
                        start=kc == 0, stop=kc == 21)
                mm.then_inc(sm.h, 1)
                sm.inc(1)
                nc.vector.wait_ge(sm.h, sm.n)
            nc.sync.dma_start(out=fino[:, 1024:2048],
                              in_=xmid_f.ap()[oc, :, gbase:gbase + 1024]).then_inc(sl.h, 16)
            sl.inc(16)
            nc.scalar.wait_ge(sm.h, sm.n)
            nc.scalar.activation(fino[:, 0:1024], pu[:, 0:1024], AF.Identity,
                                 bias=cols[:, 44 + oc:45 + oc])
            nc.scalar.drain().then_inc(ss.h, 1)
            ss.inc(1)
            nc.vector.wait_ge(ss.h, ss.n)
            nc.vector.wait_ge(sl.h, sl.n)
            nc.vector.tensor_add(fino[:, 0:1024], fino[:, 0:1024], fino[:, 1024:2048])
            nc.vector.tensor_copy(gelt[:, 0:1024], fino[:, 0:1024])
            nc.vector.drain().then_inc(sv.h, 1)
            sv.inc(1)
            nc.gpsimd.wait_ge(sv.h, sv.n)
            nc.gpsimd.dma_start(out=of32.ap()[oc, :, grp * 1024:(grp + 1) * 1024],
                                in_=fino[:, 0:1024]).then_inc(sgo.h, 16)
            sgo.inc(16)
            nc.gpsimd.dma_start(out=ob16.ap()[oc, :, grp * 1024:(grp + 1) * 1024],
                                in_=gelt[:, 0:1024]).then_inc(sgo.h, 16)
            sgo.inc(16)
            nc.vector.wait_ge(sgo.h, sgo.n)
            nc.scalar.wait_ge(sgo.h, sgo.n)
            nc.sync.wait_ge(sgo.h, sgo.n)
    nc.gpsimd.wait_ge(sgo.h, sgo.n)
    nc.gpsimd.wait_ge(sso.h, sso.n)
    es.close()
    return qkv_d, xmid_f, xmid_b


"""L0_attn device builder (full-seq rel-pos attention, qt-orientation scheme).

Scheme = core_scheme_v2 (numpy-validated). Per core: q = local [0,2048),
k/v = full ring. All per-core variation rides in flag/mask inputs.

Host inputs (nm keys):
  a_wqT/a_wkT/a_wvT/a_woT [1024,1024] f32 (W.T, [in, out])
  acols [128, 34] f32: bq 0:8, bk 8:16, bv 16:24, bo 24:32, col32 = -1.0
  relkT [64, 257] b16 (relk_rev.T)
  relvz [128, 128] b16 (relv_rev[0:128] | relv_rev[128:256]; row0-col0:64 zeroed on device)
  relv_tails [2, 64] b16 ([relv[256]; relv[0]])
  fPm/fFm/betm [16, 128, NS] f32 ; cmask [16,128,384] b16 ; smask [16,128,257] b16
Output: attn_out [8, 128, 2048] f32 d-major.
"""
import numpy as np
from contextlib import ExitStack
from concourse.ap import AP

NS = 10
NQT = 16
NH = 16
PRP_W = 640


def subranges(t):
    q0 = 128 * t
    b0 = (q0 - 128) % T
    b1 = (q0 + 256) % T
    cuts = {b0, b1}
    out = []
    for j in range(8):
        w0 = 512 * j
        local_cuts = sorted({0, 512} | {c - w0 for c in cuts if w0 < c < w0 + 512})
        for c0, c1 in zip(local_cuts[:-1], local_cuts[1:]):
            k0 = w0 + c0
            off = (k0 - b0) % T
            out.append((j, c0, c1, off if off < 384 else None))
    assert len(out) <= NS, (t, len(out))
    return out


def build_l0attn(cx, xnb, attn_out, nm, dbg=False):
    nc = cx.nc
    g = lambda n: cx.dram[nm[n]]
    wqT, wkT, wvT, woT = g("a_wqT"), g("a_wkT"), g("a_wvT"), g("a_woT")
    acols_d, relkT_d, relvz_d = g("acols"), g("relkT"), g("relvz")
    rtails_d = g("relv_tails")
    fPm_d, fFm_d, betm_d = g("fPm"), g("fFm"), g("betm")
    cmask_d, smask_d = g("cmask"), g("smask")

    qd = cx.dt("at_qd", [E, HALF], BF16)
    kd = cx.dt("at_kd", [E, T], BF16)
    vd = cx.dt("at_vd", [E, T], BF16)
    p2rp = cx.dt("at_p2rp", [NH, HALF, PRP_W], BF16)
    slabd = cx.dt("at_slabd", [NQT, 128, 384], BF16)
    concd = cx.dt("at_concd", [E, HALF], BF16)

    sl = cx.sem("at_l")
    sgo = cx.sem("at_go")
    sm = cx.sem("at_m")
    ss = cx.sem("at_s")
    sv = cx.sem("at_v")
    sp = cx.sem("at_p")
    sso = cx.sem("at_so")

    es = ExitStack()
    S = lambda n, sh, dt_: es.enter_context(nc.sbuf_tensor(n, sh, dt_))
    Pf = lambda n, sh: es.enter_context(nc.psum_tensor(n, sh, F32))
    Pb = lambda n, sh: es.enter_context(nc.psum_tensor(n, sh, BF16))

    wqb = S("at_wqb", [128, 8 * 1024], BF16)
    wkb = S("at_wkb", [128, 8 * 1024], BF16)
    wvb = S("at_wvb", [128, 8 * 1024], BF16)
    wob = S("at_wob", [128, 8 * 1024], BF16)
    wst = [S(f"at_wst{i}", [128, 1024], F32) for i in range(2)]
    acols = S("at_acols", [128, 34], F32)
    acq8 = S("at_acq8", [128, 8], F32)
    relkT = S("at_relkT", [64, 257], BF16)
    relvz = S("at_relvz", [128, 128], BF16)
    rtails = S("at_rtails", [2, 64], BF16)
    fPm = S("at_fPm", [128, NQT * NS], F32)
    fFm = S("at_fFm", [128, NQT * NS], F32)
    betm = S("at_betm", [128, NQT * NS], F32)
    cmask = S("at_cmask", [128, NQT * 384], BF16)
    smask = S("at_smask", [128, NQT * 257], BF16)
    idb = S("at_idb", [128, 128], BF16)
    idf = S("at_idf", [128, 128], F32)
    ones64 = S("at_o64", [64, 1], BF16)
    onesr = S("at_or", [1, 128], BF16)
    xst = S("at_xst", [128, 8 * 512], BF16)
    pstg = [S(f"at_pstg{i}", [128, 512], BF16) for i in range(2)]
    ostg = [S(f"at_ostg{i}", [128, 512], F32) for i in range(2)]
    qh = S("at_qh", [64, HALF], BF16)
    kh = S("at_kh", [64, T], BF16)
    vh = S("at_vh", [64, T], BF16)
    qsqb = S("at_qsq", [64, HALF], BF16)
    ksqb = S("at_ksq", [64, T], BF16)
    vts = S("at_vts", [128, 32 * 64], BF16)
    p2s = S("at_p2s", [128, 257], F32)
    p2pad = S("at_p2pad", [128, PRP_W], BF16)
    p2P = S("at_p2P", [128, NQT], F32)
    p2F = S("at_p2F", [128, NQT], F32)
    pmaxc = S("at_pmax", [128, NQT], F32)
    kn2 = S("at_kn2", [128, 32], F32)
    kmx = S("at_kmx", [128, 1], F32)
    kro = S("at_kro", [1, 128], F32)
    km1 = S("at_km1", [1, 1], BF16)
    kk1 = S("at_kk1", [128, 1], F32)
    negK = S("at_negK", [128, 1], F32)
    qn = S("at_qn", [128, NQT], F32)
    npm = S("at_npm", [128, NQT], F32)
    t1c = S("at_t1c", [128, NQT], F32)
    bqneg = S("at_bqneg", [128, NQT], F32)
    biasc = S("at_biasc", [128, NS], F32)
    biascA = S("at_biascA", [128, NS], F32)
    cmbA = S("at_cmbA", [128, NQT * NS], F32)
    cmbB = S("at_cmbB", [128, NQT * NS], F32)
    wsums = S("at_wsums", [128, NS], F32)
    junk = S("at_junk", [128, 512], F32)
    ex = S("at_ex", [128, T], BF16)
    atst = S("at_atst", [128, 32 * 128], BF16)
    skt = S("at_skt", [128, 512], BF16)
    slabex = S("at_slabex", [128, 384], BF16)
    stripb = S("at_stripb", [128, 257], BF16)
    stripm = S("at_stripm", [128, 257], BF16)
    stT = S("at_stT", [128, 256], BF16)
    tcols = S("at_tcols", [128, 8], F32)
    tailpair = S("at_tailp", [128, 2], F32)
    ttrow = S("at_ttrow", [2, 128], BF16)
    invd = S("at_invd", [128, 1], F32)
    invrow = S("at_invrow", [1, 128], BF16)
    attcb = S("at_attcb", [64, 128], BF16)
    ibcsb = S("at_ibcsb", [64, 128], F32)
    cst = S("at_cst", [128, 8 * 512], BF16)

    sp0 = Pf("at_sp0", [128, 512])
    sp1 = Pf("at_sp1", [128, 512])
    p2ps = Pf("at_p2ps", [128, 257])
    npks = Pf("at_npks", [128, 48])
    trpsM = Pb("at_trpsM", [128, 256])
    tinv = Pf("at_tinv", [2, 256])
    wibc = Pf("at_wibc", [64, 256])

    nc.sync.dma_start(out=idb[:], in_=nc.inline_tensor(bfarr(np.eye(128, dtype=np.float32)), "c_at_idb").ap()).then_inc(sl.h, 16)
    sl.inc(16)
    nc.sync.dma_start(out=idf[:], in_=nc.inline_tensor(f32arr(np.eye(128, dtype=np.float32)), "c_at_idf").ap()).then_inc(sl.h, 16)
    sl.inc(16)
    nc.sync.dma_start(out=acols[:], in_=acols_d.ap()).then_inc(sl.h, 16)
    sl.inc(16)
    nc.sync.dma_start(out=relkT[:], in_=relkT_d.ap()).then_inc(sl.h, 16)
    sl.inc(16)
    nc.sync.dma_start(out=relvz[:], in_=relvz_d.ap()).then_inc(sl.h, 16)
    sl.inc(16)
    nc.sync.dma_start(out=rtails[:], in_=rtails_d.ap()).then_inc(sl.h, 16)
    sl.inc(16)
    for src_d, dst, wd in [(fPm_d, fPm, NS), (fFm_d, fFm, NS), (betm_d, betm, NS),
                           (cmask_d, cmask, 384), (smask_d, smask, 257)]:
        for t_ in range(NQT):
            nc.sync.dma_start(out=dst[:, t_ * wd:(t_ + 1) * wd],
                              in_=src_d.ap()[t_]).then_inc(sl.h, 16)
            sl.inc(16)
    nc.vector.wait_ge(sl.h, sl.n)
    nc.vector.memset(ones64[:], 1.0)
    nc.vector.memset(onesr[:], 1.0)
    nc.vector.memset(relvz[0:1, 0:64], 0.0)
    nc.vector.tensor_scalar_mul(acq8[:], acols[:, 0:8], 0.125)
    nc.vector.drain().then_inc(sv.h, 1)
    sv.inc(1)

    for wi, (src, dst) in enumerate([(wqT, wqb), (wkT, wkb), (wvT, wvb), (woT, wob)]):
        for kc in range(8):
            pp = (wi * 8 + kc) % 2
            if wi * 8 + kc >= 2:
                nc.sync.wait_ge(sv.h, sv.n)
            nc.sync.dma_start(out=wst[pp][:], in_=src.ap()[kc * 128:(kc + 1) * 128, :]).then_inc(sl.h, 16)
            sl.inc(16)
            nc.vector.wait_ge(sl.h, sl.n)
            nc.vector.tensor_copy(dst[:, kc * 1024:(kc + 1) * 1024], wst[pp][:])
            nc.vector.drain().then_inc(sv.h, 1)
            sv.inc(1)

    nc.scalar.wait_ge(sv.h, sv.n)
    for tch in range(8):
        t0 = tch * 512
        nc.sync.wait_ge(sv.h, sv.n)
        nc.sync.wait_ge(sm.h, sm.n)
        for kc in range(8):
            nc.sync.dma_start(out=xst[:, kc * 512:(kc + 1) * 512],
                              in_=xnb.ap()[kc, :, t0:t0 + 512]).then_inc(sl.h, 16)
            sl.inc(16)
        nc.tensor.wait_ge(sl.h, sl.n)
        projs = [(wqb, qd, 0.125, acq8, 0, True), (wkb, kd, 1.0, acols, 8, False),
                 (wvb, vd, 1.0, acols, 16, False)]
        for (wb, dstd, scale_, bcol, bco, qonly) in projs:
            if qonly and tch >= 4:
                continue
            for oc in range(8):
                pg = [sp0, sp1][oc % 2]
                if ss.n > 0:
                    nc.tensor.wait_ge(ss.h, ss.n - 1)
                for kc in range(8):
                    mm = nc.tensor.matmul(pg[:, 0:512],
                                          wb[:, kc * 1024 + oc * 128:kc * 1024 + (oc + 1) * 128],
                                          xst[:, kc * 512:(kc + 1) * 512],
                                          start=kc == 0, stop=kc == 7)
                mm.then_inc(sm.h, 1)
                sm.inc(1)
                nc.scalar.wait_ge(sm.h, sm.n)
                nc.scalar.wait_ge(sso.h, sso.n)
                nc.scalar.activation(pstg[oc % 2][:], pg[:, 0:512], AF.Identity,
                                     scale=scale_, bias=bcol[:, bco + oc:bco + oc + 1])
                nc.scalar.drain().then_inc(ss.h, 1)
                ss.inc(1)
                nc.sync.wait_ge(ss.h, ss.n)
                nc.sync.dma_start(out=dstd.ap()[oc * 128:(oc + 1) * 128, t0:t0 + 512],
                                  in_=pstg[oc % 2][:]).then_inc(sso.h, 16)
                sso.inc(16)
    nc.tensor.wait_ge(sso.h, sso.n)

    subr = {t: subranges(t) for t in range(NQT)}
    for h in range(NH):
        hb = h * 64
        nc.sync.wait_ge(sm.h, sm.n)
        nc.sync.wait_ge(sv.h, sv.n)
        nc.sync.dma_start(out=qh[:], in_=qd.ap()[hb:hb + 64, :]).then_inc(sl.h, 16)
        sl.inc(16)
        nc.sync.dma_start(out=kh[:], in_=kd.ap()[hb:hb + 64, :]).then_inc(sl.h, 16)
        sl.inc(16)
        nc.sync.dma_start(out=vh[:], in_=vd.ap()[hb:hb + 64, :]).then_inc(sl.h, 16)
        sl.inc(16)
        nc.tensor.wait_ge(sl.h, sl.n)
        for kk in range(32):
            pb_c = (kk % 2) * 128
            nc.tensor.wait_ge(sv.h, sv.n)
            nc.tensor.transpose(trpsM[:, pb_c:pb_c + 64], vh[:, kk * 128:(kk + 1) * 128], idb[0:64, 0:64]).then_inc(sm.h, 1)
            sm.inc(1)
            nc.vector.wait_ge(sm.h, sm.n)
            nc.vector.tensor_copy(vts[:, kk * 64:(kk + 1) * 64], trpsM[:, pb_c:pb_c + 64])
            nc.vector.drain().then_inc(sv.h, 1)
            sv.inc(1)
            nc.tensor.wait_ge(sv.h, sv.n)
        for t in range(NQT):
            q0 = 128 * t
            nc.tensor.wait_ge(sso.h, sso.n)
            nc.tensor.wait_ge(sv.h, sv.n)
            nc.tensor.matmul(p2ps[:, 0:257], qh[:, q0:q0 + 128], relkT[:, 0:257],
                             start=True, stop=True).then_inc(sm.h, 1)
            sm.inc(1)
            nc.vector.wait_ge(sm.h, sm.n)
            nc.vector.tensor_copy(p2s[:], p2ps[:, 0:257])
            nc.vector.tensor_copy(p2P[:, t:t + 1], p2ps[:, 0:1])
            nc.vector.tensor_copy(p2F[:, t:t + 1], p2ps[:, 256:257])
            nc.vector.reduce_max(pmaxc[:, t:t + 1], p2ps[:, 0:257], axis=AX.X,
                                 apply_absolute_value=True)
            nc.vector.drain().then_inc(sv.h, 1)
            sv.inc(1)
            nc.vector.wait_ge(sv.h, sv.n)
            nc.vector.wait_ge(sso.h, sso.n)
            nc.vector.tensor_copy(p2pad[:, 192:449], p2s[:, 0:257])
            nc.vector.tensor_copy(p2pad[:, 0:192], p2s[:, 0:1].to_broadcast([128, 192]))
            nc.vector.tensor_copy(p2pad[:, 449:PRP_W], p2s[:, 256:257].to_broadcast([128, PRP_W - 449]))
            nc.vector.drain().then_inc(sv.h, 1)
            sv.inc(1)
            nc.sync.wait_ge(sv.h, sv.n)
            nc.sync.dma_start(out=p2rp.ap()[h, q0:q0 + 128, :], in_=p2pad[:]).then_inc(sso.h, 16)
            sso.inc(16)
        nc.vector.tensor_mul(qsqb[:], qh[:], qh[:])
        nc.vector.tensor_mul(ksqb[:], kh[:], kh[:])
        nc.vector.drain().then_inc(sv.h, 1)
        sv.inc(1)
        nc.tensor.wait_ge(sv.h, sv.n)
        for t in range(NQT):
            mm = nc.tensor.matmul(npks[:, t:t + 1], qsqb[:, t * 128:(t + 1) * 128],
                                  ones64[:], start=True, stop=True)
        for kk in range(32):
            mm = nc.tensor.matmul(npks[:, 16 + kk:17 + kk], ksqb[:, kk * 128:(kk + 1) * 128],
                                  ones64[:], start=True, stop=True)
        mm.then_inc(sm.h, 1)
        sm.inc(1)
        nc.vector.wait_ge(sm.h, sm.n)
        nc.vector.tensor_copy(kn2[:], npks[:, 16:48])
        nc.vector.drain().then_inc(sv.h, 1)
        sv.inc(1)
        nc.vector.wait_ge(sv.h, sv.n)
        nc.vector.drain().then_inc(sv.h, 1)
        sv.inc(1)
        nc.vector.wait_ge(sv.h, sv.n)
        nc.vector.reduce_max(kmx[:], kn2[:], axis=AX.X)
        nc.vector.drain().then_inc(sv.h, 1)
        sv.inc(1)
        nc.tensor.wait_ge(sv.h, sv.n)
        nc.tensor.transpose(tinv[0:1, 128:256], kmx[:], idf[:]).then_inc(sm.h, 1)
        sm.inc(1)
        nc.vector.wait_ge(sm.h, sm.n)
        nc.vector.tensor_copy(kro[:], tinv[0:1, 128:256])
        nc.vector.drain().then_inc(sv.h, 1)
        sv.inc(1)
        nc.vector.wait_ge(sv.h, sv.n)
        nc.vector.drain().then_inc(sv.h, 1)
        sv.inc(1)
        nc.vector.wait_ge(sv.h, sv.n)
        nc.vector.reduce_max(km1[:], kro[:], axis=AX.X)
        nc.vector.drain().then_inc(sv.h, 1)
        sv.inc(1)
        nc.tensor.wait_ge(sv.h, sv.n)
        nc.tensor.matmul(npks[:, 16:17], onesr[:], km1[:],
                         start=True, stop=True).then_inc(sm.h, 1)
        sm.inc(1)
        nc.scalar.wait_ge(sm.h, sm.n)
        nc.scalar.activation(kk1[:], npks[:, 16:17], AF.Sqrt)
        nc.scalar.activation(qn[:], npks[:, 0:NQT], AF.Sqrt)
        nc.scalar.activation(npm[:], pmaxc[:], AF.Identity, scale=-1.0,
                             bias=acols[:, 32:33])
        nc.scalar.drain().then_inc(ss.h, 1)
        ss.inc(1)
        nc.vector.wait_ge(ss.h, ss.n)
        nc.vector.tensor_scalar_mul(negK[:], kk1[:], -1.05)
        nc.vector.drain().then_inc(sv.h, 1)
        sv.inc(1)
        nc.scalar.wait_ge(sv.h, sv.n)
        nc.scalar.activation(t1c[:], qn[:], AF.Identity, scale=negK[:, 0:1])
        nc.scalar.drain().then_inc(ss.h, 1)
        ss.inc(1)
        nc.vector.wait_ge(ss.h, ss.n)
        nc.vector.tensor_add(bqneg[:], t1c[:], npm[:])
        nc.vector.drain().then_inc(sv.h, 1)
        sv.inc(1)
        for t_ in range(NQT):
            nc.vector.tensor_scalar(cmbA[:, t_ * NS:(t_ + 1) * NS],
                                    fPm[:, t_ * NS:(t_ + 1) * NS],
                                    p2P[:, t_:t_ + 1], None, ALU.mult)
        nc.vector.drain().then_inc(sv.h, 1)
        sv.inc(1)
        nc.vector.wait_ge(sv.h, sv.n)
        nc.vector.drain().then_inc(sv.h, 1)
        sv.inc(1)
        nc.vector.wait_ge(sv.h, sv.n)
        for t_ in range(NQT):
            nc.vector.scalar_tensor_tensor(cmbB[:, t_ * NS:(t_ + 1) * NS],
                                           fFm[:, t_ * NS:(t_ + 1) * NS],
                                           p2F[:, t_:t_ + 1],
                                           cmbA[:, t_ * NS:(t_ + 1) * NS],
                                           ALU.mult, ALU.add)
        nc.vector.drain().then_inc(sv.h, 1)
        sv.inc(1)
        nc.vector.wait_ge(sv.h, sv.n)
        nc.vector.drain().then_inc(sv.h, 1)
        sv.inc(1)

        nc.sync.wait_ge(sso.h, sso.n)
        for t in range(NQT):
            q0 = 128 * t
            srt = subr[t]
            nc.vector.wait_ge(ss.h, ss.n)
            nc.vector.tensor_add(biasc[:, 0:NS], cmbB[:, t * NS:(t + 1) * NS],
                                 bqneg[:, t:t + 1].to_broadcast([128, NS]))
            nc.vector.memset(wsums[:], 0.0)
            nc.vector.drain().then_inc(sv.h, 1)
            sv.inc(1)
            nc.scalar.wait_ge(sv.h, sv.n)
            for j in range(8):
                pg = [sp0, sp1][j % 2]
                nc.tensor.wait_ge(ss.h, ss.n)
                nc.tensor.wait_ge(sv.h, sv.n)
                mm = nc.tensor.matmul(pg[:, 0:512], qh[:, q0:q0 + 128],
                                      kh[:, 512 * j:512 * (j + 1)],
                                      start=True, stop=True)
                mm.then_inc(sm.h, 1)
                sm.inc(1)
                pieces = [(si, c0, c1, sb) for si, (jj, c0, c1, sb) in enumerate(srt)
                          if jj == j and sb is not None]
                for (si, c0, c1, sb) in pieces:
                    k0r = (512 * j + c0) % T
                    ln_ = c1 - c0
                    if t == 0 and k0r >= 2048:
                        base = k0r - 3968 + 192
                        apr = AP(p2rp, h * HALF * PRP_W + 0 * PRP_W + base,
                                 [[PRP_W - 1, 128], [1, ln_]])
                    else:
                        apr = AP(p2rp, h * HALF * PRP_W + q0 * (PRP_W - 1) + 320 + k0r,
                                 [[PRP_W - 1, 128], [1, ln_]])
                    nc.sync.wait_ge(sm.h, sm.n)
                    nc.sync.wait_ge(sv.h, sv.n)
                    nc.sync.dma_start(out=skt[:, c0:c1], in_=apr).then_inc(sl.h, 16)
                    sl.inc(16)
                    nc.vector.wait_ge(sl.h, sl.n)
                    nc.vector.wait_ge(sm.h, sm.n)
                    nc.vector.scalar_tensor_tensor(pg[:, c0:c1], skt[:, c0:c1],
                                                   betm[:, t * NS + si:t * NS + si + 1],
                                                   pg[:, c0:c1], ALU.mult, ALU.add)
                    nc.vector.drain().then_inc(sv.h, 1)
                    sv.inc(1)
                nc.scalar.wait_ge(sm.h, sm.n)
                nc.scalar.wait_ge(sv.h, sv.n)
                for si, (jj, c0, c1, sb) in enumerate(srt):
                    if jj != j:
                        continue
                    nc.scalar.activation(ex[:, 512 * j + c0:512 * j + c1], pg[:, c0:c1],
                                         AF.Exp, bias=biasc[:, si:si + 1],
                                         accum_out=wsums[:, si:si + 1])
                nc.scalar.drain().then_inc(ss.h, 1)
                ss.inc(1)
            nc.vector.wait_ge(ss.h, ss.n)
            nc.vector.reduce_sum(tcols[:, 0:1], wsums[:, 0:NS], axis=AX.X)
            nc.vector.tensor_mul(junk[:, 0:NS], wsums[:, 0:NS], fFm[:, t * NS:(t + 1) * NS])
            nc.vector.drain().then_inc(sv.h, 1)
            sv.inc(1)
            nc.vector.wait_ge(sv.h, sv.n)
            nc.vector.drain().then_inc(sv.h, 1)
            sv.inc(1)
            nc.vector.wait_ge(sv.h, sv.n)
            nc.vector.reduce_sum(tcols[:, 1:2], junk[:, 0:NS], axis=AX.X)
            pieces = [(si, jj, c0, c1, sb) for si, (jj, c0, c1, sb) in enumerate(srt)
                      if sb is not None]
            for (si, jj, c0, c1, sb) in pieces:
                nc.vector.tensor_copy(slabex[:, sb:sb + (c1 - c0)],
                                      ex[:, 512 * jj + c0:512 * jj + c1])
            nc.vector.drain().then_inc(sv.h, 1)
            sv.inc(1)
            nc.vector.wait_ge(sv.h, sv.n)
            nc.vector.tensor_mul(junk[:, 0:384], slabex[:], cmask[:, t * 384:(t + 1) * 384])
            nc.vector.reduce_sum(tcols[:, 2:3], junk[:, 0:384], axis=AX.X)
            nc.vector.drain().then_inc(sv.h, 1)
            sv.inc(1)
            nc.gpsimd.wait_ge(sv.h, sv.n)
            nc.gpsimd.dma_start(out=slabd.ap()[t], in_=slabex[:]).then_inc(sgo.h, 16)
            sgo.inc(16)
            nc.sync.wait_ge(sgo.h, sgo.n)
            nc.sync.dma_start(out=stripb[:],
                              in_=AP(slabd, t * 128 * 384, [[385, 128], [1, 257]])).then_inc(sl.h, 16)
            sl.inc(16)
            nc.vector.wait_ge(sl.h, sl.n)
            nc.vector.tensor_mul(stripm[:], stripb[:], smask[:, t * 257:(t + 1) * 257])
            nc.vector.drain().then_inc(sv.h, 1)
            sv.inc(1)
            nc.vector.wait_ge(sv.h, sv.n)
            nc.vector.reduce_sum(tcols[:, 3:4], stripm[:, 1:256], axis=AX.X)
            nc.vector.drain().then_inc(sv.h, 1)
            sv.inc(1)
            nc.scalar.wait_ge(sv.h, sv.n)
            nc.scalar.activation(tcols[:, 4:5], tcols[:, 3:4], AF.Identity, scale=-1.0,
                                 bias=tcols[:, 0:1])
            nc.scalar.activation(tailpair[:, 1:2], tcols[:, 1:2], AF.Identity,
                                 bias=tcols[:, 2:3])
            nc.scalar.drain().then_inc(ss.h, 1)
            ss.inc(1)
            nc.vector.wait_ge(ss.h, ss.n)
            nc.vector.tensor_sub(tailpair[:, 0:1], tcols[:, 4:5], tailpair[:, 1:2])
            nc.vector.drain().then_inc(sv.h, 1)
            sv.inc(1)
            nc.scalar.wait_ge(sv.h, sv.n)
            nc.scalar.activation(tcols[:, 5:6], tcols[:, 0:1], AF.Ln)
            nc.scalar.drain().then_inc(ss.h, 1)
            ss.inc(1)
            nc.scalar.wait_ge(ss.h, ss.n)
            nc.scalar.activation(invd[:], tcols[:, 5:6], AF.Exp, scale=-1.0)
            nc.scalar.drain().then_inc(ss.h, 1)
            ss.inc(1)
            nc.tensor.wait_ge(sv.h, sv.n)
            nc.tensor.wait_ge(ss.h, ss.n)
            nc.tensor.transpose(tinv[0:2, 0:128], tailpair[:], idf[:]).then_inc(sm.h, 1)
            sm.inc(1)
            nc.tensor.transpose(tinv[0:1, 128:256], invd[:], idf[:]).then_inc(sm.h, 1)
            sm.inc(1)
            nc.vector.wait_ge(sm.h, sm.n)
            nc.vector.tensor_copy(ttrow[:], tinv[0:2, 0:128])
            nc.vector.tensor_copy(invrow[:], tinv[0:1, 128:256])
            nc.vector.drain().then_inc(sv.h, 1)
            sv.inc(1)
            nc.tensor.wait_ge(sv.h, sv.n)
            for uc in range(2):
                nc.tensor.transpose(trpsM[:, uc * 128:(uc + 1) * 128],
                                    stripm[:, uc * 128:(uc + 1) * 128], idb[:]).then_inc(sm.h, 1)
                sm.inc(1)
            nc.vector.wait_ge(sm.h, sm.n)
            nc.vector.tensor_copy(stT[:, 0:128], trpsM[:, 0:128])
            nc.vector.tensor_copy(stT[:, 128:256], trpsM[:, 128:256])
            nc.vector.drain().then_inc(sv.h, 1)
            sv.inc(1)
            for kk in range(32):
                pb_c = (kk % 2) * 128
                nc.tensor.wait_ge(sv.h, sv.n)
                nc.tensor.wait_ge(ss.h, ss.n)
                nc.tensor.wait_ge(sp.h, sp.n)
                nc.tensor.transpose(trpsM[:, pb_c:pb_c + 128], ex[:, kk * 128:(kk + 1) * 128], idb[:]).then_inc(sm.h, 1)
                sm.inc(1)
                eng = kk % 2
                dst_sl = atst[:, kk * 128:(kk + 1) * 128]
                if eng == 0:
                    nc.vector.wait_ge(sm.h, sm.n)
                    nc.vector.tensor_copy(dst_sl, trpsM[:, pb_c:pb_c + 128])
                    nc.vector.drain().then_inc(sv.h, 1)
                    sv.inc(1)
                elif eng == 1:
                    nc.scalar.wait_ge(sm.h, sm.n)
                    nc.scalar.activation(dst_sl, trpsM[:, pb_c:pb_c + 128], AF.Identity)
                    nc.scalar.drain().then_inc(ss.h, 1)
                    ss.inc(1)
                else:
                    nc.scalar.wait_ge(sm.h, sm.n)
                    nc.scalar.activation(dst_sl, trpsM[:, pb_c:pb_c + 128], AF.Identity)
                    nc.scalar.drain().then_inc(ss.h, 1)
                    ss.inc(1)
            nc.tensor.wait_ge(sv.h, sv.n)
            nc.tensor.wait_ge(ss.h, ss.n)
            nc.tensor.wait_ge(sp.h, sp.n)
            for kk in range(32):
                nc.tensor.matmul(wibc[:, 0:128], vts[:, kk * 64:(kk + 1) * 64],
                                 atst[:, kk * 128:(kk + 1) * 128],
                                 start=kk == 0, stop=False)
            for uc in range(2):
                nc.tensor.matmul(wibc[:, 0:128], relvz[:, uc * 64:(uc + 1) * 64],
                                 stT[:, uc * 128:(uc + 1) * 128], start=False, stop=False)
            mm = nc.tensor.matmul(wibc[:, 0:128], rtails[:], ttrow[:],
                                  start=False, stop=True)
            nc.tensor.matmul(wibc[:, 128:256], onesr[0:1, 0:64], invrow[:],
                             start=True, stop=True).then_inc(sm.h, 1)
            sm.inc(1)
            nc.vector.wait_ge(sm.h, sm.n)
            nc.vector.wait_ge(sgo.h, sgo.n)
            nc.vector.tensor_copy(ibcsb[:], wibc[:, 128:256])
            nc.vector.tensor_mul(attcb[:], wibc[:, 0:128], ibcsb[:])
            nc.vector.drain().then_inc(sv.h, 1)
            sv.inc(1)
            nc.gpsimd.wait_ge(sv.h, sv.n)
            nc.gpsimd.dma_start(out=concd.ap()[hb:hb + 64, q0:q0 + 128],
                                in_=attcb[:]).then_inc(sgo.h, 16)
            sgo.inc(16)
        if dbg and h == 0:
            for nm2, tile in [("d_ex0", ex[:]), ("d_ws0", wsums[:]), ("d_bq0", bqneg[:]),
                              ("d_strip0", stripm[:]), ("d_tp0", tailpair[:]), ("d_tc0", tcols[:]),
                              ("d_p2s0", p2s[:])]:
                dd = cx.dt(nm2, list(tile.shape), tile.dtype, kind="ExternalOutput")
                nc.gpsimd.wait_ge(sv.h, sv.n)
                nc.gpsimd.dma_start(out=dd.ap(), in_=tile).then_inc(sgo.h, 16)
                sgo.inc(16)
            nc.sync.wait_ge(sgo.h, sgo.n)

    nc.sync.wait_ge(sgo.h, sgo.n)
    for tch in range(4):
        t0 = tch * 512
        nc.sync.wait_ge(sm.h, sm.n)
        for ch in range(8):
            nc.sync.dma_start(out=cst[:, ch * 512:(ch + 1) * 512],
                              in_=concd.ap()[ch * 128:(ch + 1) * 128, t0:t0 + 512]).then_inc(sl.h, 16)
            sl.inc(16)
        nc.tensor.wait_ge(sl.h, sl.n)
        for oc in range(8):
            pg = [sp0, sp1][oc % 2]
            nc.tensor.wait_ge(ss.h, ss.n)
            for ch in range(8):
                mm = nc.tensor.matmul(pg[:, 0:512],
                                      wob[:, ch * 1024 + oc * 128:ch * 1024 + (oc + 1) * 128],
                                      cst[:, ch * 512:(ch + 1) * 512],
                                      start=ch == 0, stop=ch == 7)
            mm.then_inc(sm.h, 1)
            sm.inc(1)
            nc.scalar.wait_ge(sm.h, sm.n)
            nc.scalar.wait_ge(sso.h, sso.n)
            nc.scalar.activation(ostg[oc % 2][:], pg[:, 0:512], AF.Identity,
                                 bias=acols[:, 24 + oc:24 + oc + 1])
            nc.scalar.drain().then_inc(ss.h, 1)
            ss.inc(1)
            nc.sync.wait_ge(ss.h, ss.n)
            nc.sync.dma_start(out=attn_out.ap()[oc, :, t0:t0 + 512],
                              in_=ostg[oc % 2][:]).then_inc(sso.h, 16)
            sso.inc(16)
    nc.sync.wait_ge(sso.h, sso.n)
    nc.vector.wait_ge(sso.h, sso.n)



H = 16
D = 64
MAXREL = 128


def true_delta(start, q_loc, k_loc):
    q_true = start + q_loc
    k_true = (start + k_loc) % T
    return q_true - k_true


def slab_cols(t):
    q0 = 128 * t
    return (q0 - 128 + np.arange(384)) % T


def make_attn_flags(start):
    fP = {}; fF = {}; beta = {}
    cmask = np.zeros((NQT, 128, 384), np.float32)
    smask = np.zeros((NQT, 128, 257), np.float32)
    a_ix = np.arange(128)
    for t in range(NQT):
        q0 = 128 * t
        scols = slab_cols(t)
        dt_slab = np.array([[true_delta(start, q0 + a, scols[c]) for c in range(384)]
                            for a in range(128)])
        cmask[t] = (dt_slab <= -MAXREL).astype(np.float32)
        u_ix = np.arange(257)
        cc = a_ix[:, None] + u_ix[None, :]
        ccc = np.clip(cc, 0, 383)
        dt_strip = np.take_along_axis(dt_slab, ccc, axis=1)
        smask[t] = ((np.abs(dt_strip) <= 127) & (cc <= 383)).astype(np.float32)
        beta0_cols = []
        for si, (j, c0, c1, sb) in enumerate(subranges(t)):
            k0 = 512 * j + c0
            dts = np.array([[true_delta(start, q0 + a, (k0 + c) % T)
                             for c in range(c1 - c0)] for a in (0, 127)])
            dmin, dmax = dts.min(), dts.max()
            key = (t, si)
            if sb is not None:
                if dmax <= -MAXREL:
                    fP[key], fF[key], beta[key] = 0.0, 1.0, 0.0
                    beta0_cols.append((sb, sb + (c1 - c0)))
                elif dmin >= MAXREL:
                    fP[key], fF[key], beta[key] = 1.0, 0.0, 0.0
                    beta0_cols.append((sb, sb + (c1 - c0)))
                else:
                    assert dmin > -512 and dmax < 512, (t, si, dmin, dmax)
                    fP[key], fF[key], beta[key] = 0.0, 0.0, 1.0
            else:
                if dmax <= -MAXREL:
                    fP[key], fF[key] = 0.0, 1.0
                elif dmin >= MAXREL:
                    fP[key], fF[key] = 1.0, 0.0
                else:
                    raise AssertionError((t, si, dmin, dmax))
                beta[key] = 0.0
        for (s0, s1) in beta0_cols:
            cmask[t][:, s0:s1] = 0.0
    return fP, fF, beta, cmask, smask



"""Host-side input preparation (layout only + host-computed constants)."""
import numpy as np
import ml_dtypes

T = 4096
E = 1024
HALF = 2048
FD = 64
WINDOW = 512

bf = lambda x: np.ascontiguousarray(np.asarray(x, dtype=ml_dtypes.bfloat16))
f32 = lambda x: np.ascontiguousarray(np.asarray(x), dtype=np.float32)


def dmaj(a):
    t = a.shape[0]
    return np.ascontiguousarray(a.reshape(t, 8, 128).transpose(1, 2, 0))


def undmaj(a):
    return np.ascontiguousarray(a.transpose(2, 0, 1).reshape(a.shape[2], E))


def l0ffn_layer_inputs(P, li, start):
    wqkv, q_scale, k_scale, wo, ln_g, ln_b, w1, b1, w2, b2 = [np.asarray(p[li], np.float32) for p in P]
    HID = 2730
    HP = 2816
    a_w = np.zeros((E, HP), np.float32)
    g_w = np.zeros((E, HP), np.float32)
    a_w[:, :HID] = w1[:, :HID]
    g_w[:, :HID] = w1[:, HID:]
    a_b = np.zeros(HP, np.float32)
    g_b = np.zeros(HP, np.float32)
    a_b[:HID] = b1[:HID]
    g_b[:HID] = b1[HID:]
    w1p = np.zeros((E, 2 * HP), np.float32)
    b1p = np.zeros(2 * HP, np.float32)
    for blk in range(22):
        w1p[:, blk * 256:blk * 256 + 128] = a_w[:, blk * 128:(blk + 1) * 128]
        w1p[:, blk * 256 + 128:blk * 256 + 256] = g_w[:, blk * 128:(blk + 1) * 128]
        b1p[blk * 256:blk * 256 + 128] = a_b[blk * 128:(blk + 1) * 128]
        b1p[blk * 256 + 128:blk * 256 + 256] = g_b[blk * 128:(blk + 1) * 128]
    w2p = np.zeros((HP, E), np.float32)
    w2p[:HID] = w2
    cols = np.zeros((128, 66), np.float32)
    cols[:, 63] = 1e-24
    cols[:, 64] = 1e-5
    for j in range(44):
        cols[:, j] = b1p[j * 128:(j + 1) * 128]
    for oc in range(8):
        cols[:, 44 + oc] = b2[oc * 128:(oc + 1) * 128]
        cols[:, 52 + oc] = ln_g[oc * 128:(oc + 1) * 128]
    cols[:, 60] = np.tile(q_scale, 2)
    cols[:, 61] = np.tile(k_scale, 2)
    cols[:, 62] = 0.0 if start == 0 else 1.0
    lnb = np.zeros((128, 8), np.float32)
    for oc in range(8):
        lnb[:, oc] = ln_b[oc * 128:(oc + 1) * 128]
    half_idx = np.arange(0, FD, 2, dtype=np.float32)
    inv_freq = 1.0 / 10000.0 ** (half_idx / FD)
    t = np.arange(T, dtype=np.float32)
    freqs = t[:, None] * inv_freq[None, :]
    freqs = np.concatenate([freqs, freqs], axis=-1)
    base = (half_idx + 0.4 * FD) / (1.4 * FD)
    power = (t - T // 2) / float(WINDOW // 2)
    scale = base[None, :] ** power[:, None]
    scale = np.concatenate([scale, scale], axis=-1)
    cosf, sinf = np.cos(freqs), np.sin(freqs)
    sgn = np.concatenate([-np.ones(32, np.float32), np.ones(32, np.float32)])
    qc = cosf * scale
    qs_ = sinf * scale * sgn[None, :]
    kc = cosf / scale
    ks_ = sinf / scale * sgn[None, :]
    perm = (np.arange(T) + start) % T
    rot = np.zeros((4, T, 128), np.float32)
    for i, tab in enumerate((qc, qs_, kc, ks_)):
        tl = tab[perm]
        rot[i] = np.concatenate([tl, tl], axis=1)
    W_LIST = [7, 0, 1, 2, 3]
    QTILES = [31] + list(range(16))
    mq = np.zeros((17, 128, 1024), np.float32)
    mt = np.zeros((17, 8, 128, 128), np.float32)
    for qi, qt in enumerate(QTILES):
        wq_ = (qt * 128) // 512
        kw0 = (wq_ - 1) % 8
        spans = [(7 * 512, 512), (0, 512)] if kw0 == 7 else [(kw0 * 512, 1024)]
        klocs = np.concatenate([np.arange(s0, s0 + ln) for (s0, ln) in spans])
        qlocs = qt * 128 + np.arange(128)
        ti = (start + qlocs) % T
        tj = (start + klocs) % T
        allowed = (ti[:, None] >= tj[None, :]) & (ti[:, None] <= tj[None, :] + WINDOW)
        m = np.where(allowed, 0.0, -1e30).astype(np.float32)
        mq[qi] = m
        for c8 in range(8):
            mt[qi, c8] = m[:, c8 * 128:(c8 + 1) * 128].T
    return dict(
        wqkv=f32(wqkv), wo=f32(wo), w1p=f32(w1p), w2p=f32(w2p),
        cols=f32(cols), lnb=f32(lnb), rot=bf(rot), mq=bf(mq), mt=bf(mt),
    )


def l0attn_inputs(l0a, start):
    Wq, bq, Wk, bk, Wv, bv, Wo, bo, relk, relv = [np.asarray(p, np.float32) for p in l0a]
    acols = np.zeros((128, 34), np.float32)
    for oc in range(8):
        acols[:, oc] = bq[oc * 128:(oc + 1) * 128]
        acols[:, 8 + oc] = bk[oc * 128:(oc + 1) * 128]
        acols[:, 16 + oc] = bv[oc * 128:(oc + 1) * 128]
        acols[:, 24 + oc] = bo[oc * 128:(oc + 1) * 128]
    acols[:, 32] = -1.0
    relkT = bf(relk[::-1].T.copy())
    rv_rev = relv[::-1]
    relvz = np.zeros((128, 128), np.float32)
    relvz[:, 0:64] = rv_rev[0:128]
    relvz[:, 64:128] = rv_rev[128:256]
    relv_tails = bf(np.stack([relv[256], relv[0]]))
    fP, fF, beta, cmask, smask = make_attn_flags(start)
    fPm = np.zeros((16, 128, NS), np.float32)
    fFm = np.zeros((16, 128, NS), np.float32)
    betm = np.zeros((16, 128, NS), np.float32)
    for t in range(16):
        for si in range(len(subranges(t))):
            fPm[t, :, si] = fP[(t, si)]
            fFm[t, :, si] = fF[(t, si)]
            betm[t, :, si] = beta[(t, si)]
    return dict(a_wqT=f32(Wq.T), a_wkT=f32(Wk.T), a_wvT=f32(Wv.T), a_woT=f32(Wo.T),
                acols=acols, relkT=relkT, relvz=bf(relvz), relv_tails=relv_tails,
                fPm=fPm, fFm=fFm, betm=betm, cmask=bf(cmask), smask=bf(smask))





N_CORES = 8


def _starts():
    return [2048 * (c % 2) for c in range(N_CORES)]


def kernel(**inputs):
    x = np.asarray(inputs["L0_attn_hidden_states"], np.float32)
    l0a = [np.asarray(inputs[k], np.float32) for k in
           ["L0_attn_Wq", "L0_attn_bq", "L0_attn_Wk", "L0_attn_bk",
            "L0_attn_Wv", "L0_attn_bv", "L0_attn_Wo", "L0_attn_bo",
            "L0_attn_rel_k_emb", "L0_attn_rel_v_emb"]]
    P = [np.asarray(inputs[k], np.float32) for k in
         ["L0_ffn_wqkv", "L0_ffn_q_scale", "L0_ffn_k_scale", "L0_ffn_wo",
          "L0_ffn_ln_g", "L0_ffn_ln_b", "L0_ffn_w1", "L0_ffn_b1",
          "L0_ffn_w2", "L0_ffn_b2"]]
    s4 = [np.asarray(inputs[k], np.float32) for k in
          ["L1_attn_log_dt", "L1_attn_B", "L1_attn_C", "L1_attn_D"]]
    l1f = [np.asarray(inputs[k], np.float32) for k in
           ["L1_ffn_w1_w", "L1_ffn_w1_s", "L1_ffn_w2_w", "L1_ffn_w2_s",
            "L1_ffn_h1_w", "L1_ffn_h1_s", "L1_ffn_h2_w", "L1_ffn_h2_s",
            "L1_ffn_w3_w", "L1_ffn_w3_s"]]
    starts = _starts()
    xls = [np.roll(x[c // 2], -starts[c], axis=0) for c in range(N_CORES)]
    xcols = np.zeros((128, 2), np.float32)
    xcols[:, 0] = 1e-5

    nc = bass.Bass("TRN2", target_bir_lowering=False, detect_race_conditions=False)
    cx = Ctx(nc)
    xr_d = cx.dt("xr", [8, 128, T], F32, kind="ExternalInput")
    xnf_d = cx.dt("xnf", [8, 128, T], F32, kind="ExternalOutput")
    xnb_d = cx.dt("xnb", [8, 128, T], BF16, kind="ExternalOutput")
    xc_d = cx.dt("xcols", [128, 2], F32, kind="ExternalInput")
    build_xnprep(cx, xr_d, xnf_d, xnb_d, xc_d)
    ims = [dict(xr=dmaj(xls[c]), xcols=xcols) for c in range(N_CORES)]
    res = run_bass_kernel_spmd(nc, ims, core_ids=list(range(N_CORES)))
    xnf_a = [np.asarray(res.results[c]["xnf"]) for c in range(N_CORES)]
    xnb_a = [np.asarray(res.results[c]["xnb"]) for c in range(N_CORES)]

    nc = bass.Bass("TRN2", target_bir_lowering=False, detect_race_conditions=False)
    cx = Ctx(nc)
    xnb_i = cx.dt("xnb", [8, 128, T], BF16, kind="ExternalInput")
    attn_o = cx.dt("attn_out", [8, 128, HALF], F32, kind="ExternalOutput")
    ai = {s: l0attn_inputs(l0a, s) for s in (0, 2048)}
    nm = {}
    for k, v in ai[0].items():
        name = f"ai_{k}"
        nm[k] = name
        dt_ = BF16 if v.dtype == ml_dtypes.bfloat16 else F32
        cx.dt(name, list(v.shape), dt_, kind="ExternalInput")
    build_l0attn(cx, xnb_i, attn_o, nm)
    ims = []
    for c in range(N_CORES):
        im = {f"ai_{k}": v for k, v in ai[starts[c]].items()}
        im["xnb"] = xnb_a[c]
        ims.append(im)
    res = run_bass_kernel_spmd(nc, ims, core_ids=list(range(N_CORES)))
    attn_a = [np.asarray(res.results[c]["attn_out"]) for c in range(N_CORES)]

    fi = {(li, s): l0ffn_layer_inputs(P, li, s) for li in (0, 1) for s in (0, 2048)}

    def ffn_launch(li, in_b16, in_f32):
        nc = bass.Bass("TRN2", target_bir_lowering=False, detect_race_conditions=False)
        cx = Ctx(nc)
        xrb = cx.dt("xr_b16", [8, 128, T], BF16, kind="ExternalInput")
        xrf = cx.dt("xr_f32", [8, 128, T], F32, kind="ExternalInput")
        of32 = cx.dt("of32", [8, 128, T], F32, kind="ExternalOutput")
        ob16 = cx.dt("ob16", [8, 128, T], BF16, kind="ExternalOutput")
        nm = {}
        for k, v in fi[(li, 0)].items():
            name = f"l0p_{k}"
            nm[k] = name
            dt_ = BF16 if v.dtype == ml_dtypes.bfloat16 else F32
            cx.dt(name, list(v.shape), dt_, kind="ExternalInput")
        build_l0ffn_layer(cx, li, xrb, xrf, of32, ob16, nm)
        ims = []
        for c in range(N_CORES):
            im = {f"l0p_{k}": v for k, v in fi[(li, starts[c])].items()}
            im["xr_b16"] = in_b16[c]
            im["xr_f32"] = in_f32[c]
            ims.append(im)
        res = run_bass_kernel_spmd(nc, ims, core_ids=list(range(N_CORES)))
        return ([np.asarray(res.results[c]["of32"]) for c in range(N_CORES)],
                [np.asarray(res.results[c]["ob16"]) for c in range(N_CORES)])

    f1f, f1b = ffn_launch(0, xnb_a, xnf_a)
    rg_b, rg_f = [], []
    for c in range(N_CORES):
        p = c ^ 1
        rb = np.zeros((8, 128, T), ml_dtypes.bfloat16)
        rf = np.zeros((8, 128, T), np.float32)
        rb[:, :, 0:HALF] = f1b[c][:, :, 0:HALF]
        rf[:, :, 0:HALF] = f1f[c][:, :, 0:HALF]
        rb[:, :, 3064:T] = f1b[p][:, :, 1016:HALF]
        rf[:, :, 3064:T] = f1f[p][:, :, 1016:HALF]
        rg_b.append(rb)
        rg_f.append(rf)
    f2f, _ = ffn_launch(1, rg_b, rg_f)

    nc = bass.Bass("TRN2", target_bir_lowering=False, detect_race_conditions=False)
    cx = Ctx(nc)
    cxr = cx.dt("cxr", [8, 128, HALF], F32, kind="ExternalInput")
    car = cx.dt("car", [8, 128, HALF], F32, kind="ExternalInput")
    cfr = cx.dt("cfr", [8, 128, HALF], F32, kind="ExternalInput")
    x1f_d = cx.dt("x1f", [8, 128, HALF], F32, kind="ExternalOutput")
    xn1b_d = cx.dt("xn1b", [8, 128, HALF], BF16, kind="ExternalOutput")
    xc_d = cx.dt("xcols", [128, 2], F32, kind="ExternalInput")
    build_combine(cx, cxr, car, cfr, x1f_d, xn1b_d, xc_d)
    ims = []
    for c in range(N_CORES):
        ims.append(dict(cxr=np.ascontiguousarray(dmaj(xls[c])[:, :, 0:HALF]),
                        car=attn_a[c],
                        cfr=np.ascontiguousarray(f2f[c][:, :, 0:HALF]),
                        xcols=xcols))
    res = run_bass_kernel_spmd(nc, ims, core_ids=list(range(N_CORES)))
    x1_a = [np.asarray(res.results[c]["x1f"]) for c in range(N_CORES)]
    xn1b_a = [np.asarray(res.results[c]["xn1b"]) for c in range(N_CORES)]

    nc = bass.Bass("TRN2", target_bir_lowering=False, detect_race_conditions=False)
    cx = Ctx(nc)
    build_s4d(cx)
    log_dt, Bm, Cm, Dv = s4
    ims = []
    for c in range(N_CORES):
        b = c // 2
        xg = np.concatenate([undmaj(xn1b_a[2 * b]), undmaj(xn1b_a[2 * b + 1])], axis=0)
        ims.append(dict(xg=np.ascontiguousarray(xg),
                        s4_logdt=np.ascontiguousarray(log_dt.reshape(8, 128)),
                        s4_B=np.ascontiguousarray(Bm.reshape(8, 128, 64)),
                        s4_C=np.ascontiguousarray(Cm.reshape(8, 128, 64))))
    res = run_bass_kernel_spmd(nc, ims, core_ids=list(range(N_CORES)))
    ys4_a = [np.asarray(res.results[c]["ys4f"]) for c in range(N_CORES)]

    nc = bass.Bass("TRN2", target_bir_lowering=False, detect_race_conditions=False)
    cx = Ctx(nc)
    build_l1ffn(cx)
    w1, s1, w2, s2, h1, sh1, h2, sh2, w3, s3 = l1f
    scal = np.zeros((128, 160), np.float32)
    for si_, sv_ in enumerate([s1, s2, sh1, sh2]):
        scal[:, si_ * 32:(si_ + 1) * 32] = sv_.reshape(32, 128).T
    scal[:, 128:136] = s3.reshape(8, 128).T
    scal[:, 144:152] = Dv.reshape(8, 128).T
    wshared = dict(l1_w1T=np.ascontiguousarray(w1.T), l1_w2T=np.ascontiguousarray(w2.T),
                   l1_h1T=np.ascontiguousarray(h1.T), l1_h2T=np.ascontiguousarray(h2.T),
                   l1_w3T=np.ascontiguousarray(w3.T), l1_scales=scal)
    ims = []
    for c in range(N_CORES):
        im = dict(wshared)
        im["xn1b"] = xn1b_a[c]
        im["x1d"] = x1_a[c]
        im["ys4d"] = np.ascontiguousarray(ys4_a[c][:, :, starts[c]:starts[c] + HALF])
        ims.append(im)
    res = run_bass_kernel_spmd(nc, ims, core_ids=list(range(N_CORES)))
    out = np.zeros((4, T, E), np.float32)
    for c in range(N_CORES):
        out[c // 2, starts[c]:starts[c] + HALF] = undmaj(np.asarray(res.results[c]["outd"]))
    return out
